# version 03 (final)
# baseline (speedup 1.0000x reference)
import numpy as np
import ml_dtypes
from contextlib import ExitStack
import concourse.bass as bass
import concourse.mybir as mybir
from concourse.bass_utils import run_bass_kernel_spmd

F32 = mybir.dt.float32
BF16 = mybir.dt.bfloat16
F16 = mybir.dt.float16
I16 = mybir.dt.int16
U16 = mybir.dt.uint16
I32 = mybir.dt.int32
AF = mybir.ActivationFunctionType
ALU = mybir.AluOpType
BF = ml_dtypes.bfloat16

P = 128
DM = 2048
T = 2048
DH = 2048
NST = 16
TC = 512

ENGINES = ("sync", "gpsimd", "vector", "scalar", "tensor")


class Prog:

    def __init__(self, nc):
        self.nc = nc
        self.qs = {e: [] for e in ENGINES}
        self.sems = {}
        self.counts = {}
        self.sem_names = []

    def sem(self, name):
        if name not in self.counts:
            self.counts[name] = 0
            self.sem_names.append(name)
        return name

    def post(self, q, fn, inc=None, waits=(), inc2=None):
        for s, _ in waits:
            self.sem(s)
        out = None
        if inc is not None:
            s, a = inc
            self.sem(s)
            self.counts[s] += a
            out = self.counts[s]
        if inc2 is not None:
            s, a = inc2
            self.sem(s)
            self.counts[s] += a
        self.qs[q].append((tuple(waits), fn, inc, inc2))
        return out

    def val(self, sem):
        return self.counts.get(sem, 0)

    def build(self, ctx):
        nc = self.nc
        for name in self.sem_names:
            self.sems[name] = ctx.enter_context(nc.semaphore(name))
        blk = ctx.enter_context(nc.Block())
        for ename in ENGINES:
            ops = self.qs[ename]
            if not ops:
                continue

            def section(eng, ops=ops):
                for waits, fn, inc, inc2 in ops:
                    for s, v in waits:
                        eng.wait_ge(self.sems[s], v)
                    instr = fn(eng)
                    if inc is not None and instr is not None:
                        instr.then_inc(self.sems[inc[0]], inc[1])
                    if inc2 is not None and instr is not None:
                        instr.then_inc(self.sems[inc2[0]], inc2[1])
            getattr(blk, ename)(section)


def sbuf(nc, ctx, name, shape, dt):
    return ctx.enter_context(nc.sbuf_tensor(name, shape, dt))


def psum(nc, ctx, name, shape, dt=F32):
    return ctx.enter_context(nc.psum_tensor(name, shape, dt))


def build_launch1():
    nc = bass.Bass("TRN2", target_bir_lowering=False, debug=False)
    uT = nc.dram_tensor("uT", [DM, T], F32, kind="ExternalInput").ap()
    wxiT = nc.dram_tensor("wxiT", [DM, DH], F32, kind="ExternalInput").ap()
    wziT = nc.dram_tensor("wziT", [DM, DH], F32, kind="ExternalInput").ap()
    bx = nc.dram_tensor("bx", [P, DH // P], F32, kind="ExternalInput").ap()
    bz = nc.dram_tensor("bz", [P, DH // P], F32, kind="ExternalInput").ap()
    cw = nc.dram_tensor("cw", [P, DH // P * 4], F32, kind="ExternalInput").ap()
    cb = nc.dram_tensor("cb", [P, DH // P], F32, kind="ExternalInput").ap()
    wxdtT = nc.dram_tensor("wxdtT", [DH, 128], F32, kind="ExternalInput").ap()
    wxbT = nc.dram_tensor("wxbT", [DH, NST], F32, kind="ExternalInput").ap()
    wxcT = nc.dram_tensor("wxcT", [DH, NST], F32, kind="ExternalInput").ap()
    o_xc = nc.dram_tensor("o_xc", [DH, T], BF16, kind="ExternalOutput").ap()
    o_z = nc.dram_tensor("o_z", [DH, T], BF16, kind="ExternalOutput").ap()
    o_p1 = nc.dram_tensor("o_p1", [128, T], F32, kind="ExternalOutput").ap()
    o_p2 = nc.dram_tensor("o_p2", [64, T], F32, kind="ExternalOutput").ap()
    diag_dram = nc.dram_tensor("diag_dram", [64 * 129 * 128 + 256], BF16, kind="Internal").ap()

    NO = DH // 128
    NK = DM // 128
    NT = T // TC

    with ExitStack() as ctx:
        uTs = sbuf(nc, ctx, "uTs", [P, NK * T], BF16)
        wst = [sbuf(nc, ctx, f"wst{i}", [P, NK * 128], BF16) for i in range(2)]
        xpre = sbuf(nc, ctx, "xpre", [P, T + 3], BF16)
        xconv = sbuf(nc, ctx, "xconv", [P, NO * T], BF16)
        zst = [sbuf(nc, ctx, f"zst{i}", [P, TC], BF16) for i in range(2)]
        bxs = sbuf(nc, ctx, "bxs", [P, NO], F32)
        bzs = sbuf(nc, ctx, "bzs", [P, NO], F32)
        cbs = sbuf(nc, ctx, "cbs", [P, NO], F32)
        cws = sbuf(nc, ctx, "cws", [P, NO * 4], F32)
        dstage = sbuf(nc, ctx, "dstage", [P, 129], BF16)
        diags = sbuf(nc, ctx, "diags", [P, 64 * 128], BF16)
        wdts = sbuf(nc, ctx, "wdts", [P, NK * 128], BF16)
        wbs = sbuf(nc, ctx, "wbs", [P, NK * NST], BF16)
        wcs = sbuf(nc, ctx, "wcs", [P, NK * NST], BF16)
        p1s = sbuf(nc, ctx, "p1s", [P, T], F32)
        p2s = sbuf(nc, ctx, "p2s", [64, T], F32)
        ps_xz = [psum(nc, ctx, f"ps_xz{i}", [P, TC]) for i in range(2)]
        ps_cv = [psum(nc, ctx, f"ps_cv{i}", [P, TC]) for i in range(2)]
        ps_dl = psum(nc, ctx, "ps_dl", [P, TC])
        ps_b = psum(nc, ctx, "ps_b", [NST, TC])
        ps_c = psum(nc, ctx, "ps_c", [NST, TC])

        p = Prog(nc)
        LDU = "LDU"; LDM = "LDM"; LDW = "LDW"; DIA = "DIA"
        for k in range(NK):
            p.post("gpsimd", lambda e, k=k: e.dma_start(
                out=uTs[:, k * T:(k + 1) * T],
                in_=uT[k * P:(k + 1) * P, :]), inc=(LDU, 16))
        for name, dst, src in (("bx", bxs, bx), ("bz", bzs, bz), ("cb", cbs, cb),
                               ("cw", cws, cw)):
            p.post("gpsimd", lambda e, d=dst, s=src: e.dma_start(
                out=d[:], in_=s), inc=(LDM, 16))
        p.post("gpsimd", lambda e: e.dma_start(
            out=wdts[:].rearrange("p (k r) -> p k r", k=NK),
            in_=wxdtT.rearrange("(k p) r -> p k r", p=P)), inc=(LDM, 16))
        p.post("gpsimd", lambda e: e.dma_start(
            out=wbs[:].rearrange("p (k r) -> p k r", k=NK),
            in_=wxbT.rearrange("(k p) r -> p k r", p=P)), inc=(LDM, 16))
        p.post("gpsimd", lambda e: e.dma_start(
            out=wcs[:].rearrange("p (k r) -> p k r", k=NK),
            in_=wxcT.rearrange("(k p) r -> p k r", p=P)), inc=(LDM, 16))
        LDM_ALL = p.val(LDM)

        DST = "DST"
        DSD = "DSD"
        p.post("vector", lambda e: e.memset(dstage[:], 0.0), inc=(DST, 1))
        for dt_ in range(NO):
            for j in range(4):
                i = dt_ * 4 + j
                p.post("vector",
                       lambda e, dt_=dt_, j=j: e.tensor_copy(
                           out=dstage[:, 0:1], in_=cws[:, dt_ * 4 + j:dt_ * 4 + j + 1]),
                       inc=(DST, 1),
                       waits=[("LDM", LDM_ALL), ("DST", p.val(DST)), ("DSD", p.val(DSD))])
                p.post("gpsimd",
                       lambda e, i=i: e.dma_start(
                           out=diag_dram[i * 16512:(i + 1) * 16512].rearrange(
                               "(a b) -> a b", a=P), in_=dstage[:]),
                       inc=(DSD, 16), waits=[("DST", p.val(DST))])
        DSD_ALL = p.val(DSD)
        for i in range(64):
            p.post("gpsimd",
                   lambda e, i=i: e.dma_start(
                       out=diags[:, i * 128:(i + 1) * 128],
                       in_=diag_dram[i * 16512: i * 16512 + 16384].rearrange(
                           "(a b) -> a b", a=P)),
                   inc=(DIA, 16), waits=[("DSD", DSD_ALL)] if i == 0 else ())
        DIA_ALL = p.val(DIA)

        MX = "MX"; EVX = "EVX"; MCV = "MCV"; SCV = "SCV"; STO = "STO"
        MZ = "MZ"; EVZ = "EVZ"
        def w_src(oc):
            w = wxiT if oc < NO else wziT
            o = (oc % NO) * 128
            return w.rearrange("(k p) o -> p k o", p=P)[:, :, o:o + 128]
        for oc in range(2 * NO):
            buf = wst[oc % 2]
            if oc >= 2:
                pv = oc - 2
                wv = [("MX", (pv + 1) * NT)] if pv < NO else [("MZ", (pv + 1 - NO) * NT)]
            else:
                wv = []
            for q4 in range(4):
                p.post("gpsimd", lambda e, oc=oc, buf=buf, q4=q4: e.dma_start(
                    out=buf[:].rearrange("p (k o) -> p k o", k=NK)[:, q4 * 4:(q4 + 1) * 4, :],
                    in_=w_src(oc)[:, q4 * 4:(q4 + 1) * 4, :]),
                    inc=(LDW, 16), waits=wv if q4 == 0 else ())
        for oc in range(NO):
            buf = wst[oc % 2]
            for t in range(NT):
                pz = ps_xz[t % 2]
                waits = [("LDW", (oc + 1) * 64), ("LDU", 16 * NK)]
                if oc * NT + t >= 2:
                    waits.append(("EVX", oc * NT + t - 1))
                for k in range(NK):
                    p.post("tensor", lambda e, pz=pz, buf=buf, k=k, t=t: e.matmul(
                        pz[:], buf[:, k * 128:(k + 1) * 128],
                        uTs[:, k * T + t * TC: k * T + (t + 1) * TC],
                        start=(k == 0), stop=(k == NK - 1)),
                        waits=waits if k == 0 else (),
                        inc=(MX, 1) if k == NK - 1 else None)
                p.post("scalar", lambda e, pz=pz, oc=oc, t=t: e.activation(
                    out=xpre[:, 3 + t * TC: 3 + (t + 1) * TC], in_=pz[:],
                    func=AF.Identity, bias=bxs[:, oc:oc + 1]),
                    inc=(EVX, 1), waits=[("MX", oc * NT + t + 1)] +
                    ([("MCV", oc * NT)] if oc >= 1 and t == 0 else []))
            if oc == 0:
                p.post("vector", lambda e: e.memset(xpre[:, 0:3], 0.0), inc=("XP0", 1))
            for t in range(NT):
                pc = ps_cv[t % 2]
                waits = [("EVX", oc * NT + t + 1), ("DIA", DIA_ALL), ("XP0", 1)]
                if oc * NT + t >= 2:
                    waits.append(("SCV", oc * NT + t - 1))
                for j in range(4):
                    di = oc * 4 + j
                    p.post("tensor", lambda e, pc=pc, di=di, t=t, j=j: e.matmul(
                        pc[:], diags[:, di * 128:(di + 1) * 128],
                        xpre[:, t * TC + j: t * TC + j + TC],
                        start=(j == 0), stop=(j == 3)),
                        waits=waits if j == 0 else (),
                        inc=(MCV, 1) if j == 3 else None)
                p.post("scalar", lambda e, pc=pc, oc=oc, t=t: e.activation(
                    out=xconv[:, oc * T + t * TC: oc * T + (t + 1) * TC], in_=pc[:],
                    func=AF.Silu, bias=cbs[:, oc:oc + 1]),
                    inc=(SCV, 1), waits=[("MCV", oc * NT + t + 1)])
            p.post("sync", lambda e, oc=oc: e.dma_start(
                out=o_xc[oc * 128:(oc + 1) * 128, :],
                in_=xconv[:, oc * T:(oc + 1) * T]),
                inc=(STO, 16), waits=[("SCV", (oc + 1) * NT)])
        for oc in range(NO):
            buf = wst[(NO + oc) % 2]
            for t in range(NT):
                pz = ps_xz[t % 2]
                waits = [("LDW", (NO + oc + 1) * 64)]
                if oc * NT + t >= 2:
                    waits.append(("EVZ", oc * NT + t - 1))
                else:
                    waits.append(("EVX", NO * NT))
                for k in range(NK):
                    p.post("tensor", lambda e, pz=pz, buf=buf, k=k, t=t: e.matmul(
                        pz[:], buf[:, k * 128:(k + 1) * 128],
                        uTs[:, k * T + t * TC: k * T + (t + 1) * TC],
                        start=(k == 0), stop=(k == NK - 1)),
                        waits=waits if k == 0 else (),
                        inc=(MZ, 1) if k == NK - 1 else None)
                zb = zst[(oc * NT + t) % 2]
                p.post("scalar", lambda e, pz=pz, zb=zb, oc=oc: e.activation(
                    out=zb[:], in_=pz[:], func=AF.Identity, bias=bzs[:, oc:oc + 1]),
                    inc=(EVZ, 1),
                    waits=[("MZ", oc * NT + t + 1)] +
                          ([("STO", 16 * NO + (oc * NT + t - 2 + 1) * 16)]
                           if oc * NT + t >= 2 else []))
                p.post("sync", lambda e, zb=zb, oc=oc, t=t: e.dma_start(
                    out=o_z[oc * 128:(oc + 1) * 128, t * TC:(t + 1) * TC], in_=zb[:]),
                    inc=(STO, 16), waits=[("EVZ", oc * NT + t + 1)])
        MD = "MD"; EVD = "EVD"
        SCV_ALL = p.val(SCV)
        for t in range(NT):
            for k in range(NK):
                p.post("tensor", lambda e, k=k, t=t: e.matmul(
                    ps_dl[:], wdts[:, k * 128:(k + 1) * 128],
                    xconv[:, k * T + t * TC: k * T + (t + 1) * TC],
                    start=(k == 0), stop=(k == NK - 1)),
                    waits=([("SCV", SCV_ALL), ("LDM", LDM_ALL)] +
                           ([("EVD", (t - 1) * 3 + 1)] if t >= 1 else []))
                    if k == 0 else (),
                    inc=(MD, 1) if k == NK - 1 else None)
            for k in range(NK):
                p.post("tensor", lambda e, k=k, t=t: e.matmul(
                    ps_b[:], wbs[:, k * NST:(k + 1) * NST],
                    xconv[:, k * T + t * TC: k * T + (t + 1) * TC],
                    start=(k == 0), stop=(k == NK - 1)),
                    waits=[("EVD", (t - 1) * 3 + 2)] if (k == 0 and t >= 1) else (),
                    inc=(MD, 1) if k == NK - 1 else None)
            for k in range(NK):
                p.post("tensor", lambda e, k=k, t=t: e.matmul(
                    ps_c[:], wcs[:, k * NST:(k + 1) * NST],
                    xconv[:, k * T + t * TC: k * T + (t + 1) * TC],
                    start=(k == 0), stop=(k == NK - 1)),
                    waits=[("EVD", t * 3)] if (k == 0 and t >= 1) else (),
                    inc=(MD, 1) if k == NK - 1 else None)
            p.post("vector", lambda e, t=t: e.tensor_copy(
                out=p1s[:, t * TC:(t + 1) * TC], in_=ps_dl[:]),
                inc=(EVD, 1), waits=[("MD", t * 3 + 1)])
            p.post("vector", lambda e, t=t: e.tensor_copy(
                out=p2s[0:NST, t * TC:(t + 1) * TC], in_=ps_b[:]),
                inc=(EVD, 1), waits=[("MD", t * 3 + 2)])
            p.post("vector", lambda e, t=t: e.tensor_copy(
                out=p2s[32:32 + NST, t * TC:(t + 1) * TC], in_=ps_c[:]),
                inc=(EVD, 1), waits=[("MD", t * 3 + 3)])
        p.post("sync", lambda e: e.dma_start(out=o_p1, in_=p1s[:]),
               inc=(STO, 16), waits=[("EVD", p.val(EVD))])
        p.post("sync", lambda e: e.dma_start(out=o_p2, in_=p2s[:]),
               inc=(STO, 16))
        p.post("sync", lambda e: None, waits=[("STO", p.val(STO))])

        p.build(ctx)
    return nc


def percol(v):
    return np.ascontiguousarray(v.reshape(-1, P).T)


def host_inputs_launch1(inp, c):
    b, s = c // 2, c % 2
    w_in = inp["L0_attn_w_in"]
    asc = np.ascontiguousarray
    return dict(
        uT=asc(inp["L0_attn_u"][b].T),
        wxiT=asc(w_in[s * DH:(s + 1) * DH, :].T),
        wziT=asc(w_in[4096 + s * DH: 4096 + (s + 1) * DH, :].T),
        bx=percol(inp["L0_attn_b_in"][s * DH:(s + 1) * DH]),
        bz=percol(inp["L0_attn_b_in"][4096 + s * DH: 4096 + (s + 1) * DH]),
        cw=asc(inp["L0_attn_conv_w"][s * DH:(s + 1) * DH, 0, :].reshape(
            DH // P, P, 4).transpose(1, 0, 2).reshape(P, -1)),
        cb=percol(inp["L0_attn_conv_b"][s * DH:(s + 1) * DH]),
        wxdtT=asc(inp["L0_attn_w_xdt"][:, s * DH:(s + 1) * DH].T),
        wxbT=asc(inp["L0_attn_w_xb"][:, s * DH:(s + 1) * DH].T),
        wxcT=asc(inp["L0_attn_w_xc"][:, s * DH:(s + 1) * DH].T),
    )


def kernel(**inputs):
    raise NotImplementedError("assembling incrementally")


def build_launch2(debug_taps=False, nd_lim=None, ndb_lim=None):
    nc = bass.Bass("TRN2", target_bir_lowering=False, debug=False)
    xc_in = nc.dram_tensor("xc_in", [DH, T], BF16, kind="ExternalInput").ap()
    z_in = nc.dram_tensor("z_in", [DH, T], BF16, kind="ExternalInput").ap()
    p1a = nc.dram_tensor("p1a", [128, T], F32, kind="ExternalInput").ap()
    p1b = nc.dram_tensor("p1b", [128, T], F32, kind="ExternalInput").ap()
    p2a = nc.dram_tensor("p2a", [64, T], F32, kind="ExternalInput").ap()
    p2b = nc.dram_tensor("p2b", [64, T], F32, kind="ExternalInput").ap()
    wdtT = nc.dram_tensor("wdtT", [128, DH], F32, kind="ExternalInput").ap()
    bdt = nc.dram_tensor("bdt", [P, 16], F32, kind="ExternalInput").ap()
    acols = nc.dram_tensor("acols", [P, 256], F32, kind="ExternalInput").ap()
    dcols = nc.dram_tensor("dcols", [P, 16], F32, kind="ExternalInput").ap()
    e1_in = nc.dram_tensor("e1_in", [16, P, P], BF16, kind="ExternalInput").ap()
    rr_in = nc.dram_tensor("rr_in", [16, P, P], BF16, kind="ExternalInput").ap()
    woT = nc.dram_tensor("woT", [DH, DM], F32, kind="ExternalInput").ap()
    o_po = nc.dram_tensor("o_po", [T, DM], F32, kind="ExternalOutput").ap()
    if debug_taps:
        o_dt0 = nc.dram_tensor("o_dt0", [P, T], BF16, kind="ExternalOutput").ap()
        o_dtx0 = nc.dram_tensor("o_dtx0", [P, T], BF16, kind="ExternalOutput").ap()
        o_da0 = nc.dram_tensor("o_da0", [P, T], BF16, kind="ExternalOutput").ap()
        o_uu0 = nc.dram_tensor("o_uu0", [P, T], BF16, kind="ExternalOutput").ap()
        o_hh0 = nc.dram_tensor("o_hh0", [P, T], F32, kind="ExternalOutput").ap()
        o_ymix = nc.dram_tensor("o_ymix", [DH, T], BF16, kind="ExternalOutput").ap()
        o_dl = nc.dram_tensor("o_dl", [P, T], BF16, kind="ExternalOutput").ap()
        o_brep = nc.dram_tensor("o_brep", [P, T], BF16, kind="ExternalOutput").ap()
        o_ps3 = nc.dram_tensor("o_ps3", [P, TC], F32, kind="ExternalOutput").ap()
        o_sp1 = nc.dram_tensor("o_sp1", [P, TC], F32, kind="ExternalOutput").ap()
        o_rsb = nc.dram_tensor("o_rsb", [P, TC], F32, kind="ExternalOutput").ap()
        o_wdt = nc.dram_tensor("o_wdt", [P, DH], BF16, kind="ExternalOutput").ap()
    bc_dram = nc.dram_tensor("bc_dram", [64, T], F32, kind="Internal").ap()
    ymix_dram = nc.dram_tensor("ymix_dram", [DH, T], BF16, kind="Internal").ap()

    ND = 16
    NDB = 16
    NDL = nd_lim or ND
    NDBL = ndb_lim or NDB
    NT = T // TC

    with ExitStack() as ctx:
        xc = sbuf(nc, ctx, "xc", [P, ND * T], BF16)
        woTs = sbuf(nc, ctx, "woTs", [P, 16 * (DM // 4)], BF16)
        dtlowf = sbuf(nc, ctx, "dtlowf", [P, T], F32)
        dtlowb = sbuf(nc, ctx, "dtlowb", [P, T], BF16)
        bcs = sbuf(nc, ctx, "bcs", [64, T], F32)
        brep = sbuf(nc, ctx, "brep", [P, T], BF16)
        crep = sbuf(nc, ctx, "crep", [P, T], BF16)
        wdts = sbuf(nc, ctx, "wdts", [P, DH], BF16)
        bdts = sbuf(nc, ctx, "bdts", [P, 16], F32)
        acs = sbuf(nc, ctx, "acs", [P, 256], F32)
        dcs = sbuf(nc, ctx, "dcs", [P, 16], F32)
        e1s = sbuf(nc, ctx, "e1s", [P, 16 * P], BF16)
        rrs = sbuf(nc, ctx, "rrs", [P, 16 * P], BF16)
        sp1 = [sbuf(nc, ctx, f"sp1_{i}", [P, TC], F32) for i in range(2)]
        sp2 = [sbuf(nc, ctx, f"sp2_{i}", [P, TC], F32) for i in range(2)]
        sp3 = [sbuf(nc, ctx, f"sp3_{i}", [P, TC], F32) for i in range(2)]
        rsb = [sbuf(nc, ctx, f"rsb_{i}", [P, TC], F32) for i in range(2)]
        dtb1 = sbuf(nc, ctx, "dtb1", [P, T], BF16)
        dtx1 = sbuf(nc, ctx, "dtx1", [P, T], BF16)
        dtb = [dtb1, dtb1]
        dtx = [dtx1, dtx1]
        da = [sbuf(nc, ctx, f"da_{i}", [P, T], BF16) for i in range(2)]
        uu = [sbuf(nc, ctx, f"uu_{i}", [P, T], BF16) for i in range(2)]
        hh1 = sbuf(nc, ctx, "hh1", [P, T], F32)
        hh = [hh1, hh1]
        yp = [sbuf(nc, ctx, f"yp_{i}", [P, T], BF16) for i in range(2)]
        zstr = [sbuf(nc, ctx, f"zstr_{i}", [P, T], BF16) for i in range(2)]
        silz = [sbuf(nc, ctx, f"silz_{i}", [P, TC], F32) for i in range(2)]
        y2sb = [sbuf(nc, ctx, f"y2sb_{i}", [P, TC], F32) for i in range(2)]
        ymx1 = sbuf(nc, ctx, "ymx1", [P, T], BF16)
        ymx = [ymx1, ymx1]
        ycol1 = sbuf(nc, ctx, "ycol1", [P, 16 * 128], BF16)
        ycol = [ycol1, ycol1]
        post = [sbuf(nc, ctx, f"post_{i}", [P, TC], F32) for i in range(2)]
        ps_arg = [psum(nc, ctx, f"ps_arg{i}", [P, TC]) for i in range(2)]
        ps_dtx = [psum(nc, ctx, f"ps_dtx{i}", [P, TC]) for i in range(2)]
        ps_dt = [ps_arg[0], ps_arg[0]]
        ps_y = [psum(nc, ctx, f"ps_y{i}", [P, TC]) for i in range(4)]
        arg_last = {0: None, 1: None}
        dtx_last = {0: None, 1: None}

        p = Prog(nc)
        LD = "LD"
        LDXC = "LDXC"
        LDWO = "LDWO"
        SUM = "SUM"
        BCD = "BCD"
        BCR = "BCR"
        SUM2 = "SUM2"
        MDT = "MDT"; SPL = "SPL"; RV = "RV"; DTB = "DTB"; DTXC = "DTXC"
        MARG = "MARG"; SEXP = "SEXP"; MDTX = "MDTX"; UMUL = "UMUL"
        SCAN = "SCAN"; YPM = "YPM"; MY = "MY"
        LDZ = "LDZ"; SZ = "SZ"; EPI = "EPI"; YDMA = "YDMA"
        LDYC = "LDYC"; MWO = "MWO"; EVO = "EVO"; STO = "STO"

        for k in range(ND):
            p.post("gpsimd", lambda e, k=k: e.dma_start(
                out=xc[:, k * T:(k + 1) * T], in_=xc_in[k * P:(k + 1) * P, :]),
                inc=(LDXC, 16))
        LDXC_ALL = p.val(LDXC)
        p.post("sync", lambda e: e.dma_start(out=dtlowf[:], in_=p1a), inc=(LD, 16))
        p.post("sync", lambda e: e.dma_start(out=bcs[:], in_=p2a), inc=(LD, 16))
        p.post("gpsimd", lambda e: e.dma_start(out=wdts[:], in_=wdtT), inc=(LD, 16))
        p.post("sync", lambda e: e.dma_start(out=bdts[:], in_=bdt), inc=(LD, 16))
        p.post("sync", lambda e: e.dma_start(out=acs[:], in_=acols), inc=(LD, 16))
        p.post("sync", lambda e: e.dma_start(out=dcs[:], in_=dcols), inc=(LD, 16))
        p.post("sync", lambda e: e.dma_start(
            out=e1s[:].rearrange("p (a c) -> p a c", a=16),
            in_=e1_in.rearrange("a b c -> b a c")), inc=(LD, 16))
        p.post("sync", lambda e: e.dma_start(
            out=rrs[:].rearrange("p (a c) -> p a c", a=16),
            in_=rr_in.rearrange("a b c -> b a c")), inc=(LD, 16))
        LD_ALL = p.val(LD)
        p.post("gpsimd", lambda e: e.dma_start(out=dtlowf[:], in_=p1b,
                                               accum_op=ALU.add),
               inc=(SUM2, 16), waits=[("LD", LD_ALL)])
        p.post("gpsimd", lambda e: e.dma_start(out=bcs[:], in_=p2b,
                                               accum_op=ALU.add),
               inc=(SUM2, 16), waits=[("LD", LD_ALL)])
        HDM = DM // 4

        p.post("scalar", lambda e: e.activation(out=acs[:], in_=acs[:], func=AF.Exp),
               inc=("ANEG", 1), waits=[("LD", LD_ALL)])
        p.post("vector", lambda e: e.tensor_single_scalar(
            out=acs[:], in_=acs[:], scalar=-1.0, op=ALU.mult),
            inc=("ANEG", 1), waits=[("ANEG", 1)])
        p.post("vector", lambda e: e.tensor_copy(out=dtlowb[:], in_=dtlowf[:]),
               inc=(SUM, 1), waits=[("SUM2", 32)])
        p.post("vector", lambda e: e.tensor_copy(out=dtlowb[:, 0:1], in_=dtlowf[:, 0:1]),
               inc=(SUM, 1), waits=[("SUM", 1)])
        p.post("vector", lambda e: e.tensor_copy(out=dtlowb[:, 0:1], in_=dtlowf[:, 0:1]),
               inc=(SUM, 1), waits=[("SUM", 2)])
        p.post("sync", lambda e: e.dma_start(out=bc_dram, in_=bcs[:]),
               inc=(BCD, 16), waits=[("SUM2", 32)])
        for i in range(8):
            p.post("gpsimd", lambda e, i=i: e.dma_start(
                out=brep[16 * i:16 * (i + 1), :], in_=bc_dram[0:16, :]),
                inc=(BCR, 16), waits=[("BCD", 16)] if i == 0 else ())
            p.post("gpsimd", lambda e, i=i: e.dma_start(
                out=crep[16 * i:16 * (i + 1), :], in_=bc_dram[32:48, :]),
                inc=(BCR, 16))
        BCR_ALL = p.val(BCR)

        for dtile in range(NDL):
            par = dtile % 2
            for t in range(NT):
                ut = dtile * NT + t
                pd = ps_dt[0]
                w = [("SUM", 3), ("LD", LD_ALL)]
                if ut >= 1:
                    w += [("SPL", (ut - 1) * 3 + 1), ("RV", ut)]
                if arg_last[0] is not None:
                    w.append(arg_last[0])
                mdt_m = p.post("tensor", lambda e, pd=pd, dtile=dtile, t=t: e.matmul(
                    pd[:], wdts[:, dtile * 128:(dtile + 1) * 128],
                    dtlowb[:, t * TC:(t + 1) * TC], start=True, stop=True),
                    inc=(MDT, 1), waits=w)
                arg_last[0] = ("RV", p.val(RV) + 1)
                sp_w = [("MDT", ut + 1)]
                up = ut % 2
                p.post("scalar", lambda e, pd=pd, dtile=dtile, t=t, up=up: e.activation(
                    out=sp1[up][:], in_=pd[:], func=AF.Abs,
                    bias=bdts[:, dtile:dtile + 1]),
                    inc=(SPL, 1), waits=sp_w + ([("DTB", ut - 1)] if ut >= 2 else []))
                p.post("scalar", lambda e, up=up: e.activation(
                    out=sp2[up][:], in_=sp1[up][:], func=AF.Sigmoid),
                    inc=(SPL, 1), waits=[("SPL", p.val(SPL))])
                p.post("scalar", lambda e, up=up: e.activation(
                    out=sp3[up][:], in_=sp2[up][:], func=AF.Ln),
                    inc=(SPL, 1), waits=[("SPL", p.val(SPL))])
                p.post("vector", lambda e, pd=pd, dtile=dtile, up=up: e.tensor_scalar(
                    out=rsb[up][:], in0=pd[:], scalar1=bdts[:, dtile:dtile + 1],
                    scalar2=0.0, op0=ALU.add, op1=ALU.max),
                    inc=(RV, 1), waits=[("SPL", ut * 3 + 1)] +
                    ([("DTB", ut - 1)] if ut >= 2 else []))
                p.post("vector", lambda e, dtile=dtile, t=t, par=par, up=up: e.tensor_sub(
                    out=dtb[par][:, t * TC:(t + 1) * TC], in0=rsb[up][:], in1=sp3[up][:]),
                    inc=(DTB, 1), waits=[("SPL", ut * 3 + 3), ("RV", ut + 1)] +
                    ([("MARG", dtile * NDBL * NT)] if (dtile >= 1 and t == 0) else []))
                p.post("vector", lambda e, dtile=dtile, t=t, par=par: e.tensor_mul(
                    out=dtx[par][:, t * TC:(t + 1) * TC],
                    in0=dtb[par][:, t * TC:(t + 1) * TC],
                    in1=xc[:, dtile * T + t * TC: dtile * T + (t + 1) * TC]),
                    inc=(DTXC, 1), waits=[("DTB", ut + 1), ("LDXC", LDXC_ALL)] +
                    ([("MDTX", dtile * NDBL * NT)] if (dtile >= 1 and t == 0) else []))
            for db in range(NDBL):
                idx = dtile * NDBL + db
                spar = idx % 2
                for t in range(NT):
                    g = idx * NT + t
                    gp_ = g % 2
                    aw = [("DTB", dtile * NT + NT)]
                    if arg_last[gp_] is not None:
                        aw.append(arg_last[gp_])
                    p.post("tensor", lambda e, db=db, t=t, par=par, gp_=gp_: e.matmul(
                        ps_arg[gp_][:], e1s[:, db * P:(db + 1) * P],
                        dtb[par][:, t * TC:(t + 1) * TC], start=True, stop=True),
                        inc=(MARG, 1), waits=aw)
                    sm = p.post("scalar", lambda e, dtile=dtile, db=db, t=t, spar=spar, gp_=gp_: e.activation(
                        out=da[spar][:, t * TC:(t + 1) * TC], in_=ps_arg[gp_][:],
                        func=AF.Exp, scale=acs[:, dtile * 16 + db: dtile * 16 + db + 1]),
                        inc=(SEXP, 1), waits=[("MARG", g + 1), ("ANEG", 2)] +
                        ([("SCAN", idx - 1)] if (idx >= 2 and t == 0) else []))
                    arg_last[gp_] = ("SEXP", sm)
                for t in range(NT):
                    g = idx * NT + t
                    gp_ = g % 2
                    dw = [("DTXC", dtile * NT + NT)]
                    if dtx_last[gp_] is not None:
                        dw.append(dtx_last[gp_])
                    p.post("tensor", lambda e, db=db, t=t, par=par, gp_=gp_: e.matmul(
                        ps_dtx[gp_][:], e1s[:, db * P:(db + 1) * P],
                        dtx[par][:, t * TC:(t + 1) * TC], start=True, stop=True),
                        inc=(MDTX, 1), waits=dw)
                    um = p.post("vector", lambda e, t=t, spar=spar, gp_=gp_: e.tensor_tensor(
                        out=uu[spar][:, t * TC:(t + 1) * TC], in0=ps_dtx[gp_][:],
                        in1=brep[:, t * TC:(t + 1) * TC], op=ALU.mult),
                        inc=(UMUL, 1), waits=[("MDTX", g + 1), ("BCR", BCR_ALL)] +
                        ([("SCAN", idx - 1)] if (idx >= 2 and t == 0) else []))
                    dtx_last[gp_] = ("UMUL", um)
                p.post("vector", lambda e, spar=spar: e.tensor_tensor_scan(
                    out=hh[spar][:], data0=da[spar][:], data1=uu[spar][:],
                    initial=0.0, op0=ALU.mult, op1=ALU.add),
                    inc=(SCAN, 1),
                    waits=[("SEXP", (dtile * NDBL + db + 1) * NT), ("UMUL", (dtile * NDBL + db + 1) * NT)] +
                    ([("YPM", idx)] if idx >= 1 else []))
                p.post("vector", lambda e, spar=spar: e.tensor_mul(
                    out=yp[spar][:], in0=hh[spar][:], in1=crep[:]),
                    inc=(YPM, 1), waits=[("SCAN", idx + 1)] +
                    ([("MY", (idx - 1) * NT)] if idx >= 2 else []))
                for t in range(NT):
                    w = [("YPM", idx + 1)]
                    if db == 0 and dtile >= 1 and t == 0:
                        w.append(("EPI", dtile * 2 * NT))
                    p.post("tensor", lambda e, db=db, t=t, spar=spar: e.matmul(
                        ps_y[t][:], rrs[:, db * P:(db + 1) * P],
                        yp[spar][:, t * TC:(t + 1) * TC],
                        start=(db == 0), stop=(db == NDBL - 1), skip_group_check=True),
                        inc=(MY, 1), waits=w)
            zpar = dtile % 2
            p.post("gpsimd", lambda e, dtile=dtile, zpar=zpar: e.dma_start(
                out=zstr[zpar][:], in_=z_in[dtile * P:(dtile + 1) * P, :]),
                inc=(LDZ, 16),
                waits=[("SZ", (dtile - 1) * NT)] if dtile >= 2 else ())
            for t in range(NT):
                tp = t % 2
                p.post("scalar", lambda e, t=t, zpar=zpar, tp=tp: e.activation(
                    out=silz[tp][:],
                    in_=zstr[zpar][:, t * TC:(t + 1) * TC], func=AF.Silu),
                    inc=(SZ, 1), waits=[("LDZ", (dtile + 1) * 16)] +
                    ([("EPI", p.val(EPI) - 2)] if p.val(EPI) >= 2 else []))
                p.post("vector", lambda e, dtile=dtile, t=t, tp=tp: e.scalar_tensor_tensor(
                    out=y2sb[tp][:],
                    in0=xc[:, dtile * T + t * TC: dtile * T + (t + 1) * TC],
                    scalar=dcs[:, dtile:dtile + 1], in1=ps_y[t][:],
                    op0=ALU.mult, op1=ALU.add),
                    inc=(EPI, 1),
                    waits=[("MY", (dtile * NDBL + NDBL) * NT)] +
                    ([("YDMA", (dtile - 2) * 16 + 16)] if dtile >= 2 and t == 0 else []) +
                    ([("EPI", p.val(EPI) - 2)] if p.val(EPI) >= 2 else []))
                p.post("vector", lambda e, dtile=dtile, t=t, zpar=zpar, tp=tp: e.tensor_mul(
                    out=ymx[zpar][:, t * TC:(t + 1) * TC],
                    in0=y2sb[tp][:],
                    in1=silz[tp][:]),
                    inc=(EPI, 1),
                    waits=[("EPI", p.val(EPI)), ("SZ", dtile * NT + t + 1)] +
                    ([("YDMA", dtile * 16)] if (dtile >= 1 and t == 0) else []))
            p.post("sync", lambda e, dtile=dtile, zpar=zpar: e.dma_start(
                out=ymix_dram[dtile * P:(dtile + 1) * P, :], in_=ymx[zpar][:]),
                inc=(YDMA, 16), waits=[("EPI", p.val(EPI))])
        YDMA_ALL = p.val(YDMA)

        for oh in range(4):
            for k in range(ND):
                wv = []
                if oh >= 1 and k == 0:
                    wv.append(("MWO", oh * 16))
                p.post("gpsimd", lambda e, k=k, oh=oh: e.dma_start(
                    out=woTs[:, k * HDM:(k + 1) * HDM],
                    in_=woT[k * P:(k + 1) * P, oh * HDM:(oh + 1) * HDM]),
                    inc=(LDWO, 16), waits=wv)
            for tt in range(16):
                ypar = tt % 2
                for q4 in range(4):
                    yw = []
                    if oh == 0 and tt == 0 and q4 == 0:
                        yw.append(("YDMA", YDMA_ALL))
                    if tt >= 1 and q4 == 0:
                        yw.append(("MWO", oh * 16 + tt))
                    elif oh >= 1 and q4 == 0:
                        yw.append(("MWO", oh * 16))
                    p.post("gpsimd", lambda e, tt=tt, q4=q4, ypar=ypar: e.dma_start(
                        out=ycol[ypar][:].rearrange("p (k c) -> p k c", k=16)[:, q4 * 4:(q4 + 1) * 4, :],
                        in_=ymix_dram.rearrange("(k p) t -> p k t", p=P)[
                            :, q4 * 4:(q4 + 1) * 4, tt * 128:(tt + 1) * 128]),
                        inc=(LDYC, 16), waits=yw)
                for o in range(1):
                    un = oh * 16 + tt
                    pw = ps_arg[un % 2]
                    w = [("LDYC", (oh * 16 + tt + 1) * 64), ("LDWO", (oh + 1) * 16 * 16)]
                    if un >= 2:
                        w.append(("EVO", un - 1))
                    if un <= 1:
                        w.append(("RV", p.val(RV)))
                        w.append(("SPL", p.val(SPL)))
                        w.append(("SEXP", p.val(SEXP)))
                    for k in range(16):
                        p.post("tensor", lambda e, pw=pw, k=k, o=o, ypar=ypar: e.matmul(
                            pw[:], ycol[ypar][:, k * 128:(k + 1) * 128],
                            woTs[:, k * HDM + o * TC: k * HDM + (o + 1) * TC],
                            start=(k == 0), stop=(k == 15)),
                            waits=w if k == 0 else (),
                            inc=(MWO, 1) if k == 15 else None)
                    p.post("vector", lambda e, pw=pw, un=un: e.tensor_copy(
                        out=post[un % 2][:], in_=pw[:]),
                        inc=(EVO, 1), waits=[("MWO", un + 1)] +
                        ([("STO", (un - 1) * 16)] if un >= 2 else []))
                    p.post("sync", lambda e, tt=tt, o=o, un=un, oh=oh: e.dma_start(
                        out=o_po[tt * 128:(tt + 1) * 128,
                                 oh * HDM + o * TC: oh * HDM + (o + 1) * TC],
                        in_=post[un % 2][:]),
                        inc=(STO, 16), waits=[("EVO", un + 1)])
        if debug_taps:
            p.post("sync", lambda e: e.dma_start(out=o_dt0, in_=dtb[0][:]),
                   inc=(STO, 16), waits=[("DTB", 4)])
            p.post("sync", lambda e: e.dma_start(out=o_dtx0, in_=dtx[0][:]),
                   inc=(STO, 16), waits=[("DTXC", 4)])
            p.post("sync", lambda e: e.dma_start(out=o_da0, in_=da[0][:]),
                   inc=(STO, 16), waits=[("SEXP", 4)])
            p.post("sync", lambda e: e.dma_start(out=o_uu0, in_=uu[0][:]),
                   inc=(STO, 16), waits=[("UMUL", 4)])
            p.post("sync", lambda e: e.dma_start(out=o_hh0, in_=hh[0][:]),
                   inc=(STO, 16), waits=[("SCAN", 1)])
            p.post("sync", lambda e: e.dma_start(out=o_ymix, in_=ymix_dram),
                   inc=(STO, 16), waits=[("YDMA", YDMA_ALL)])
            p.post("sync", lambda e: e.dma_start(out=o_dl, in_=dtlowb[:]),
                   inc=(STO, 16), waits=[("SUM", 3)])
            p.post("sync", lambda e: e.dma_start(out=o_brep, in_=brep[:]),
                   inc=(STO, 16), waits=[("BCR", BCR_ALL)])
            p.post("vector", lambda e: e.tensor_copy(out=y2sb[0][:], in_=ps_dt[1][:]),
                   inc=("TAPV", 1), waits=[("MDT", 4)])
            p.post("sync", lambda e: e.dma_start(out=o_ps3, in_=y2sb[0][:]),
                   inc=(STO, 16), waits=[("TAPV", 1)])
            p.post("sync", lambda e: e.dma_start(out=o_sp1, in_=sp1[1][:]),
                   inc=(STO, 16), waits=[("SPL", 12)])
            p.post("sync", lambda e: e.dma_start(out=o_rsb, in_=rsb[1][:]),
                   inc=(STO, 16), waits=[("RV", 4)])
            o_sp2 = nc.dram_tensor("o_sp2", [P, TC], F32, kind="ExternalOutput").ap()
            o_sp3 = nc.dram_tensor("o_sp3", [P, TC], F32, kind="ExternalOutput").ap()
            p.post("sync", lambda e: e.dma_start(out=o_sp2, in_=sp2[1][:]),
                   inc=(STO, 16), waits=[("SPL", 12)])
            p.post("sync", lambda e: e.dma_start(out=o_sp3, in_=sp3[1][:]),
                   inc=(STO, 16), waits=[("SPL", 12)])
            p.post("sync", lambda e: e.dma_start(out=o_wdt, in_=wdts[:]),
                   inc=(STO, 16), waits=[("LD", LD_ALL)])
        p.post("sync", lambda e: None, waits=[("STO", p.val(STO))])
        p.build(ctx)
    return nc


def host_inputs_launch2(inp, c, l1outs):
    b, s = c // 2, c % 2
    asc = np.ascontiguousarray
    alog = inp["L0_attn_A_log"][s * DH:(s + 1) * DH]
    acols = alog.reshape(16, 16, 8, 16).transpose(2, 3, 0, 1).reshape(128, 256)
    dsl = inp["L0_attn_D_param"][s * DH:(s + 1) * DH]
    e1 = np.zeros((16, P, P), np.float32)
    rr = np.zeros((16, P, P), np.float32)
    for db in range(16):
        for d8 in range(8):
            for n in range(16):
                e1[db, db * 8 + d8, d8 * 16 + n] = 1.0
                rr[db, d8 * 16 + n, db * 8 + d8] = 1.0
    prt = c - 1 if s else c + 1
    return dict(
        xc_in=l1outs[c]["o_xc"], z_in=l1outs[c]["o_z"],
        p1a=l1outs[c]["o_p1"], p1b=l1outs[prt]["o_p1"],
        p2a=l1outs[c]["o_p2"], p2b=l1outs[prt]["o_p2"],
        wdtT=asc(inp["L0_attn_w_dt"][s * DH:(s + 1) * DH, :].T),
        bdt=percol(inp["L0_attn_b_dt"][s * DH:(s + 1) * DH]),
        acols=asc(acols), dcols=asc(dsl.reshape(16, P).T),
        e1_in=e1.astype(BF), rr_in=rr.astype(BF),
        woT=asc(inp["L0_attn_w_out"][:, s * DH:(s + 1) * DH].T),
    )


NGU = 44
NDN = 16


def build_launch3(parts="ABC"):
    nc = bass.Bass("TRN2", target_bir_lowering=False, debug=False)
    poa = nc.dram_tensor("poa", [1024, DM], F32, kind="ExternalInput").ap()
    pob = nc.dram_tensor("pob", [1024, DM], F32, kind="ExternalInput").ap()
    bo = nc.dram_tensor("bo", [1, DM], F32, kind="ExternalInput").ap()
    ures = nc.dram_tensor("ures", [1024, DM], F32, kind="ExternalInput").ap()
    gu_pk = nc.dram_tensor("gu_pk", [5632, 256], I32, kind="ExternalInput").ap()
    gu_sc = nc.dram_tensor("gu_sc", [5632, 512], F16, kind="ExternalInput").ap()
    dn_pk = nc.dram_tensor("dn_pk", [DM, 352], I32, kind="ExternalInput").ap()
    dn_sc = nc.dram_tensor("dn_sc", [DM, 704], F16, kind="ExternalInput").ap()
    x1o = nc.dram_tensor("x1o", [1024, DM], F32, kind="ExternalOutput").ap()
    wgu_d = nc.dram_tensor("wgu_d", [5632, DM], BF16, kind="ExternalOutput").ap()
    wdn_d = nc.dram_tensor("wdn_d", [DM, 2816], BF16, kind="ExternalOutput").ap()

    with ExitStack() as ctx:
        lnt = [sbuf(nc, ctx, f"lnt{i}", [P, DM], F32) for i in range(2)]
        lba = [sbuf(nc, ctx, f"lba{i}", [P, DM], F32) for i in range(2)]
        lbb = [sbuf(nc, ctx, f"lbb{i}", [P, DM], F32) for i in range(2)]
        borep = sbuf(nc, ctx, "borep", [P, DM], F32)
        x1t = [sbuf(nc, ctx, f"x1t{i}", [P, DM], F32) for i in range(2)]
        trash = sbuf(nc, ctx, "trash", [P, DM], F32)
        stat = sbuf(nc, ctx, "stat", [P, 8], F32)
        pk = [sbuf(nc, ctx, f"pk{i}", [P, 256], I32) for i in range(2)]
        scr = [sbuf(nc, ctx, f"scr{i}", [P, 512], F16) for i in range(2)]
        srow = sbuf(nc, ctx, "srow", [P, 512], BF16)
        q1 = sbuf(nc, ctx, "q1", [P, 512], U16)
        q2 = sbuf(nc, ctx, "q2", [P, 512], U16)
        q3 = sbuf(nc, ctx, "q3", [P, 512], U16)
        dq = [sbuf(nc, ctx, f"dq{i}", [P, DM], BF16) for i in range(2)]
        pkd = [sbuf(nc, ctx, f"pkd{i}", [P, 352], I32) for i in range(2)]
        scrd = [sbuf(nc, ctx, f"scrd{i}", [P, 704], F16) for i in range(2)]
        srowd = sbuf(nc, ctx, "srowd", [P, 704], BF16)
        qd1 = sbuf(nc, ctx, "qd1", [P, 704], U16)
        qd2 = sbuf(nc, ctx, "qd2", [P, 704], U16)
        qd3 = sbuf(nc, ctx, "qd3", [P, 704], U16)
        dqd = [sbuf(nc, ctx, f"dqd{i}", [P, 2816], BF16) for i in range(2)]

        p = Prog(nc)
        LDL = "LDL"; LNV = "LNV"; LNS = "LNS"; STX = "STX"
        LDP = "LDP"; DQV = "DQV"; STW = "STW"

        ntt = 1 if "1" in parts else (2 if "2" in parts else 8)
        p.post("gpsimd", lambda e: e.dma_start(
            out=borep[:], in_=bo.partition_broadcast(P)), inc=("LDBO", 16))
        for tt in range(ntt):
            par = tt % 2
            wv = [("LNV", (tt - 1) * 9)] if tt >= 2 else []
            p.post("gpsimd", lambda e, tt=tt, par=par: e.dma_start(
                out=lnt[par][:], in_=poa[tt * P:(tt + 1) * P, :]),
                inc=(LDL, 16), waits=wv)
            p.post("gpsimd", lambda e, tt=tt, par=par: e.dma_start(
                out=lba[par][:], in_=pob[tt * P:(tt + 1) * P, :]), inc=(LDL, 16))
            p.post("gpsimd", lambda e, tt=tt, par=par: e.dma_start(
                out=lbb[par][:], in_=ures[tt * P:(tt + 1) * P, :]), inc=(LDL, 16))
            ld_done = p.val(LDL)
            p.post("vector", lambda e, par=par: e.tensor_add(
                out=lnt[par][:], in0=lnt[par][:], in1=lba[par][:]),
                inc=(LNV, 1), waits=[("LDL", ld_done), ("LDBO", 16)])
            p.post("vector", lambda e, par=par: e.tensor_add(
                out=lnt[par][:], in0=lnt[par][:], in1=lbb[par][:]),
                inc=(LNV, 1), waits=[("LNV", p.val(LNV))])
            p.post("vector", lambda e, par=par: e.tensor_add(
                out=lnt[par][:], in0=lnt[par][:], in1=borep[:]),
                inc=(LNV, 1), waits=[("LNV", p.val(LNV))])
            p.post("vector", lambda e, par=par: e.reduce_sum(
                stat[:, 0:1], lnt[par][:], axis=mybir.AxisListType.X),
                inc=(LNV, 1), waits=[("LNV", p.val(LNV))])
            p.post("vector", lambda e: e.tensor_single_scalar(
                out=stat[:, 1:2], in_=stat[:, 0:1], scalar=-1.0 / DM, op=ALU.mult),
                inc=(LNV, 1), waits=[("LNV", p.val(LNV))])
            p.post("scalar", lambda e, par=par: e.activation(
                out=trash[:], in_=lnt[par][:], func=AF.Square, bias=stat[:, 1:2]),
                inc=(LNS, 1), waits=[("LNV", p.val(LNV))] +
                ([("LNV", (tt - 1) * 9 + 6)] if tt >= 1 else []))
            p.post("vector", lambda e: e.reduce_sum(
                stat[:, 2:3], trash[:], axis=mybir.AxisListType.X),
                inc=(LNV, 1), waits=[("LNS", p.val(LNS))])
            p.post("vector", lambda e: e.tensor_scalar(
                out=stat[:, 3:4], in0=stat[:, 2:3], scalar1=1.0 / DM, scalar2=1e-5,
                op0=ALU.mult, op1=ALU.add),
                inc=(LNV, 1), waits=[("LNV", p.val(LNV))])
            p.post("scalar", lambda e: e.activation(
                out=stat[:, 4:5], in_=stat[:, 3:4], func=AF.Ln),
                inc=(LNS, 1), waits=[("LNV", p.val(LNV))])
            p.post("vector", lambda e: e.tensor_single_scalar(
                out=stat[:, 5:6], in_=stat[:, 4:5], scalar=-0.5, op=ALU.mult),
                inc=(LNV, 1), waits=[("LNS", p.val(LNS))])
            p.post("scalar", lambda e: e.activation(
                out=stat[:, 6:7], in_=stat[:, 5:6], func=AF.Exp),
                inc=(LNS, 1), waits=[("LNV", p.val(LNV))])
            p.post("vector", lambda e, par=par: e.tensor_scalar(
                out=x1t[par][:], in0=lnt[par][:], scalar1=stat[:, 1:2],
                scalar2=stat[:, 6:7], op0=ALU.add, op1=ALU.mult),
                inc=(LNV, 1), waits=[("LNS", p.val(LNS))] +
                ([("STX", (tt - 2) * 16 + 16)] if tt >= 2 else []))
            p.post("sync", lambda e, tt=tt, par=par: e.dma_start(
                out=x1o[tt * P:(tt + 1) * P, :], in_=x1t[par][:]),
                inc=(STX, 16), waits=[("LNV", p.val(LNV))])

        def dequant_tiles(ntile, npkw, pk_dram, sc_dram, out_dram, pk_b, scr_b,
                          srow_b, qa, qb, qc, dq_b, ncols, tag):
            LDp = "LDP" + tag; DQ = "DQV" + tag; STw = "STW" + tag
            nm = npkw * 2
            for i in range(ntile):
                par = i % 2
                p.post("gpsimd", lambda e, i=i, par=par: e.dma_start(
                    out=pk_b[par][:], in_=pk_dram[i * P:(i + 1) * P, :]),
                    inc=(LDp, 16), waits=[(DQ, (i - 1) * 25)] if i >= 2 else [])
                p.post("gpsimd", lambda e, i=i, par=par: e.dma_start(
                    out=scr_b[par][:], in_=sc_dram[i * P:(i + 1) * P, :]),
                    inc=(LDp, 16))
                ld = p.val(LDp)
                p.post("vector", lambda e, par=par: e.tensor_single_scalar(
                    out=srow_b[:], in_=scr_b[par][:], scalar=float(2.0 ** 126),
                    op=ALU.mult),
                    inc=(DQ, 1), waits=[(LDp, ld)] +
                    ([(DQ, i * 25)] if i >= 1 else []))
                for j in range(4):
                    v16 = pk_b[par][:].bitcast(U16)
                    sh1 = 6 - 4 * j
                    sh2 = 12 - 4 * j
                    base = p.val(DQ)
                    if sh1 >= 0:
                        p.post("vector", lambda e, v16=v16, sh1=sh1: e.tensor_single_scalar(
                            out=q1[:, :nm] if tag == "g" else qa[:, :nm], in_=v16,
                            scalar=sh1, op=ALU.logical_shift_left),
                            inc=(DQ, 1), waits=[(DQ, base)])
                    else:
                        p.post("vector", lambda e, v16=v16, sh1=sh1: e.tensor_single_scalar(
                            out=q1[:, :nm] if tag == "g" else qa[:, :nm], in_=v16,
                            scalar=-sh1, op=ALU.logical_shift_right),
                            inc=(DQ, 1), waits=[(DQ, base)])
                    p.post("vector", lambda e, v16=v16, sh2=sh2: e.tensor_single_scalar(
                        out=q2[:, :nm] if tag == "g" else qb[:, :nm], in_=v16,
                        scalar=sh2, op=ALU.logical_shift_left),
                        inc=(DQ, 1), waits=[(DQ, base)])
                    qq1 = q1 if tag == "g" else qa
                    qq2 = q2 if tag == "g" else qb
                    qq3 = q3 if tag == "g" else qc
                    p.post("vector", lambda e, qq1=qq1: e.tensor_single_scalar(
                        out=qq1[:, :nm], in_=qq1[:, :nm], scalar=0x01C0, op=ALU.bitwise_and),
                        inc=(DQ, 1), waits=[(DQ, p.val(DQ))])
                    p.post("vector", lambda e, qq2=qq2: e.tensor_single_scalar(
                        out=qq2[:, :nm], in_=qq2[:, :nm], scalar=0x8000, op=ALU.bitwise_and),
                        inc=(DQ, 1), waits=[(DQ, p.val(DQ))])
                    p.post("vector", lambda e, qq1=qq1, qq2=qq2, qq3=qq3: e.tensor_tensor(
                        out=qq3[:, :nm], in0=qq1[:, :nm], in1=qq2[:, :nm], op=ALU.bitwise_or),
                        inc=(DQ, 1), waits=[(DQ, p.val(DQ))])
                    p.post("vector", lambda e, qq3=qq3, j=j, par=par, nc_=ncols: e.tensor_tensor(
                        out=dq_b[par][:, j: j + 4 * (nc_ - 1) + 1: 4].rearrange("p (a b) -> p a b", b=1),
                        in0=qq3[:, :nm].bitcast(BF16).rearrange("p (a b) -> p a b", b=1),
                        in1=srow_b[:, :nm].rearrange("p (a b) -> p a b", b=1),
                        op=ALU.mult),
                        inc=(DQ, 1), waits=[(DQ, p.val(DQ))] +
                        ([(STw, (i - 2) * 16 + 16)] if (i >= 2 and j == 0) else []))
                p.post("sync", lambda e, i=i, par=par: e.dma_start(
                    out=out_dram[i * P:(i + 1) * P, :], in_=dq_b[par][:]),
                    inc=(STw, 16), waits=[(DQ, p.val(DQ))])
            p.post("sync", lambda e: None, waits=[(STw, p.val(STw))])

        if "B" in parts:
            dequant_tiles(NGU, 256, gu_pk, gu_sc, wgu_d, pk, scr, srow,
                          q1, q2, q3, dq, 512, "g")
        if "C" in parts:
            dequant_tiles(NDN, 352, dn_pk, dn_sc, wdn_d, pkd, scrd, srowd,
                          qd1, qd2, qd3, dqd, 704, "d")
        p.post("sync", lambda e: None, waits=[("STX", p.val("STX"))])
        p.build(ctx)
    return nc


def host_inputs_launch3(inp, c, l2outs):
    b, s = c // 2, c % 2
    asc = np.ascontiguousarray
    prt = c - 1 if s else c + 1
    tsl = slice(s * 1024, (s + 1) * 1024)
    gsel = []
    for cix in range(22):
        gsel.append(np.arange(s * 2816 + cix * 128, s * 2816 + (cix + 1) * 128))
        gsel.append(np.arange(5632 + s * 2816 + cix * 128, 5632 + s * 2816 + (cix + 1) * 128))
    gsel = np.concatenate(gsel)
    gu_pk = inp["L0_ffn_gate_up_packed"][gsel]
    gu_sc = np.repeat(inp["L0_ffn_gate_up_scales"].T[gsel], 32, axis=1)
    dn_pk = inp["L0_ffn_down_packed"][:, s * 352:(s + 1) * 352]
    dn_sc = np.repeat(inp["L0_ffn_down_scales"][s * 22:(s + 1) * 22].T, 32, axis=1)
    return dict(
        poa=np.ascontiguousarray(l2outs[c]["o_po"][tsl]),
        pob=np.ascontiguousarray(l2outs[prt]["o_po"][tsl]),
        bo=asc(inp["L0_attn_b_out"][None, :]),
        ures=asc(inp["L0_attn_u"][b][tsl]),
        gu_pk=asc(gu_pk), gu_sc=asc(gu_sc.astype(np.float16)),
        dn_pk=asc(dn_pk), dn_sc=asc(dn_sc.astype(np.float16)),
    )


def build_launch4():
    nc = bass.Bass("TRN2", target_bir_lowering=False, debug=False)
    x1a = nc.dram_tensor("x1a", [1024, DM], F32, kind="ExternalInput").ap()
    x1b = nc.dram_tensor("x1b", [1024, DM], F32, kind="ExternalInput").ap()
    wgu = nc.dram_tensor("wgu", [5632, DM], BF16, kind="ExternalInput").ap()
    wdn = nc.dram_tensor("wdn", [DM, 2816], BF16, kind="ExternalInput").ap()
    o_fp = nc.dram_tensor("o_fp", [T, DM], F32, kind="ExternalOutput").ap()
    x1bf_dram = nc.dram_tensor("x1bf_dram", [T, DM], BF16, kind="Internal").ap()
    wguT_dram = nc.dram_tensor("wguT_dram", [DM, 5632], BF16, kind="Internal").ap()

    NC22 = 22

    with ExitStack() as ctx:
        xstg = [sbuf(nc, ctx, f"xstg{i}", [P, DM], BF16) for i in range(2)]
        x1T = sbuf(nc, ctx, "x1T", [P, 16 * 1024], BF16)
        hsb = sbuf(nc, ctx, "hsb", [P, NC22 * 1024], BF16)
        wdnT = sbuf(nc, ctx, "wdnT", [P, NC22 * DM], BF16)
        wstr = [sbuf(nc, ctx, f"wstr{i}", [P, 16 * 256], BF16) for i in range(2)]
        hgst = [sbuf(nc, ctx, f"hgst{i}", [P, TC], BF16) for i in range(2)]
        post4 = [sbuf(nc, ctx, f"post4{i}", [P, TC], F32) for i in range(2)]
        ps_g = [psum(nc, ctx, f"ps_g{i}", [P, TC]) for i in range(2)]
        ps_u = [psum(nc, ctx, f"ps_u{i}", [P, TC]) for i in range(2)]
        ps_d = [psum(nc, ctx, f"ps_d{i}", [P, TC]) for i in range(2)]

        p = Prog(nc)
        XC = "XC"; XW = "XW"; RT = "RT"; RW = "RW"; WD = "WD"
        XT = "XT"; LW = "LW"; MG = "MG"; MU = "MU"; SCS = "SCS"; HV = "HV"
        MD = "MD"; EV = "EV"; STO = "STO"

        for i in range(16):
            src = x1a if i < 8 else x1b
            row = (i % 8) * P
            par = i % 2
            wv = [("XW", (i - 1) * 16)] if i >= 2 else []
            p.post("gpsimd", lambda e, src=src, row=row, par=par: e.dma_start(
                out=xstg[par][:], in_=src[row:row + P, :]),
                inc=(XC, 16), waits=wv)
            p.post("gpsimd", lambda e, i=i, par=par: e.dma_start(
                out=x1bf_dram[i * P:(i + 1) * P, :], in_=xstg[par][:]),
                inc=(XW, 16), waits=[("XC", p.val(XC))])
        XW_ALL = p.val(XW)

        for k in range(16):
            p.post("sync", lambda e, k=k: e.dma_start_transpose(
                out=wdnT[:, 0:5632], in_=wgu[:, k * P:(k + 1) * P]),
                inc=(RT, 16), waits=[("RW", k * 16)])
            p.post("sync", lambda e, k=k: e.dma_start(
                out=wguT_dram[k * P:(k + 1) * P, :], in_=wdnT[:, 0:5632]),
                inc=(RW, 16), waits=[("RT", (k + 1) * 16)])
        RW_ALL = p.val(RW)
        for c in range(NC22):
            p.post("sync", lambda e, c=c: e.dma_start_transpose(
                out=wdnT[:, c * DM:(c + 1) * DM], in_=wdn[:, c * P:(c + 1) * P]),
                inc=(WD, 16), waits=[("RW", RW_ALL)] if c == 0 else ())
        WD_ALL = p.val(WD)

        for ts in range(2):
            for k in range(16):
                wv = [("XW", XW_ALL), ("WD", WD_ALL)]
                if ts == 1:
                    wv.append(("MU", ts * NC22 * 2))
                p.post("sync", lambda e, k=k, ts=ts: e.dma_start_transpose(
                    out=x1T[:, k * 1024:(k + 1) * 1024],
                    in_=x1bf_dram[ts * 1024:(ts + 1) * 1024, k * P:(k + 1) * P]),
                    inc=(XT, 16), waits=wv if k == 0 else ())
            XT_TS = p.val(XT)
            for c in range(NC22):
                par = c % 2
                cw = [("RW", RW_ALL)]
                gi = ts * NC22 + c
                if gi >= 2:
                    cw.append(("MU", (gi - 1) * 2))
                p.post("gpsimd", lambda e, c=c, par=par: e.dma_start(
                    out=wstr[par][:].rearrange("p (k o) -> p k o", k=16),
                    in_=wguT_dram.rearrange("(kc pp) o -> pp kc o", pp=P)[
                        :, :, c * 256:(c + 1) * 256]),
                    inc=(LW, 16), waits=cw)
                for tc in range(2):
                    un = (ts * NC22 + c) * 2 + tc
                    pg = ps_g[tc]
                    pu = ps_u[tc]
                    w0 = [("LW", (ts * NC22 + c + 1) * 16), ("XT", XT_TS)]
                    if un >= 2:
                        w0.append(("SCS", un - 1))
                    for k in range(16):
                        p.post("tensor", lambda e, pg=pg, par=par, k=k, tc=tc: e.matmul(
                            pg[:], wstr[par][:, k * 256:k * 256 + 128],
                            x1T[:, k * 1024 + tc * TC: k * 1024 + (tc + 1) * TC],
                            start=(k == 0), stop=(k == 15)),
                            waits=w0 if k == 0 else (),
                            inc=(MG, 1) if k == 15 else None)
                    w1 = [("HV", un - 1)] if un >= 2 else []
                    for k in range(16):
                        p.post("tensor", lambda e, pu=pu, par=par, k=k, tc=tc: e.matmul(
                            pu[:], wstr[par][:, k * 256 + 128:(k + 1) * 256],
                            x1T[:, k * 1024 + tc * TC: k * 1024 + (tc + 1) * TC],
                            start=(k == 0), stop=(k == 15)),
                            waits=w1 if k == 0 else (),
                            inc=(MU, 1) if k == 15 else None)
                    p.post("scalar", lambda e, pg=pg, un=un: e.activation(
                        out=hgst[un % 2][:], in_=pg[:], func=AF.Silu),
                        inc=(SCS, 1), waits=[("MG", un + 1)])
                    p.post("vector", lambda e, pu=pu, un=un, c=c, tc=tc: e.tensor_tensor(
                        out=hsb[:, c * 1024 + tc * TC: c * 1024 + (tc + 1) * TC],
                        in0=hgst[un % 2][:], in1=pu[:], op=ALU.mult),
                        inc=(HV, 1), waits=[("SCS", un + 1), ("MU", un + 1)] +
                        ([("MD", ts * 32)] if (ts == 1 and un == ts * NC22 * 2) else []))
            for tt in range(8):
                for od in range(4):
                    un = (ts * 32) + tt * 4 + od
                    pd = ps_d[un % 2]
                    w0 = [("HV", (ts + 1) * NC22 * 2)]
                    if un >= 2:
                        w0.append(("EV", un - 1))
                    for c in range(NC22):
                        p.post("tensor", lambda e, pd=pd, c=c, tt=tt, od=od: e.matmul(
                            pd[:], hsb[:, c * 1024 + tt * 128: c * 1024 + (tt + 1) * 128],
                            wdnT[:, c * DM + od * TC: c * DM + (od + 1) * TC],
                            start=(c == 0), stop=(c == NC22 - 1)),
                            waits=w0 if c == 0 else (),
                            inc=(MD, 1) if c == NC22 - 1 else None)
                    p.post("vector", lambda e, pd=pd, un=un: e.tensor_copy(
                        out=post4[un % 2][:], in_=pd[:]),
                        inc=(EV, 1), waits=[("MD", un + 1)] +
                        ([("STO", (un - 1) * 16)] if un >= 2 else []))
                    p.post("sync", lambda e, ts=ts, tt=tt, od=od, un=un: e.dma_start(
                        out=o_fp[ts * 1024 + tt * 128: ts * 1024 + (tt + 1) * 128,
                                 od * TC:(od + 1) * TC],
                        in_=post4[un % 2][:]),
                        inc=(STO, 16), waits=[("EV", un + 1)])
        p.post("sync", lambda e: None, waits=[("STO", p.val(STO))])
        p.build(ctx)
    return nc


def host_inputs_launch4(inp, c, l3outs):
    b, s = c // 2, c % 2
    even, odd = 2 * (c // 2), 2 * (c // 2) + 1
    return dict(
        x1a=l3outs[even]["x1o"], x1b=l3outs[odd]["x1o"],
        wgu=l3outs[c]["wgu_d"], wdn=l3outs[c]["wdn_d"],
    )


def build_launch5():
    nc = bass.Bass("TRN2", target_bir_lowering=False, debug=False)
    fpa = nc.dram_tensor("fpa", [1024, DM], F32, kind="ExternalInput").ap()
    fpb = nc.dram_tensor("fpb", [1024, DM], F32, kind="ExternalInput").ap()
    x1own = nc.dram_tensor("x1own", [1024, DM], F32, kind="ExternalInput").ap()
    wqkvT = nc.dram_tensor("wqkvT", [DM, 6144], F32, kind="ExternalInput").ap()
    bqk = nc.dram_tensor("bqk", [P, 32], F32, kind="ExternalInput").ap()
    bv = nc.dram_tensor("bv", [1, DM], F32, kind="ExternalInput").ap()
    x2o = nc.dram_tensor("x2o", [1024, DM], F32, kind="ExternalOutput").ap()
    qo = nc.dram_tensor("qo", [DM, 1024], BF16, kind="ExternalOutput").ap()
    ko = nc.dram_tensor("ko", [DM, 1024], BF16, kind="ExternalOutput").ap()
    vo = nc.dram_tensor("vo", [1024, DM], BF16, kind="ExternalOutput").ap()
    x2bf_dram = nc.dram_tensor("x2bf_dram", [1024, DM], BF16, kind="Internal").ap()

    with ExitStack() as ctx:
        lnt = [sbuf(nc, ctx, f"lnt{i}", [P, DM], F32) for i in range(2)]
        lba = [sbuf(nc, ctx, f"lba{i}", [P, DM], F32) for i in range(2)]
        lbb = [sbuf(nc, ctx, f"lbb{i}", [P, DM], F32) for i in range(2)]
        x2t = [sbuf(nc, ctx, f"x2t{i}", [P, DM], F32) for i in range(2)]
        x2b = [sbuf(nc, ctx, f"x2b{i}", [P, DM], BF16) for i in range(2)]
        trash = sbuf(nc, ctx, "trash", [P, DM], F32)
        stat = sbuf(nc, ctx, "stat", [P, 16], F32)
        x2T = sbuf(nc, ctx, "x2T", [P, 16 * 1024], BF16)
        wv = sbuf(nc, ctx, "wv", [P, 16 * DM], BF16)
        wstr = [sbuf(nc, ctx, f"wstr{i}", [P, 16 * 128], BF16) for i in range(2)]
        bqs = sbuf(nc, ctx, "bqs", [P, 32], F32)
        bvrep = sbuf(nc, ctx, "bvrep", [P, DM], F32)
        qkst = [sbuf(nc, ctx, f"qkst{i}", [P, TC], BF16) for i in range(2)]
        vst = [sbuf(nc, ctx, f"vst{i}", [P, TC], BF16) for i in range(2)]
        ps_qk = [psum(nc, ctx, f"ps_qk{i}", [P, TC]) for i in range(2)]
        ps_v = [psum(nc, ctx, f"ps_v{i}", [P, TC]) for i in range(2)]

        p = Prog(nc)
        LD = "LD"; LNV = "LNV"; LNS = "LNS"; STX = "STX"; XW = "XW"
        XT = "XT"; LWV = "LWV"; LW = "LW"; MQ = "MQ"; SQK = "SQK"; STQ = "STQ"
        MV = "MV"; VV = "VV"; STV = "STV"

        p.post("gpsimd", lambda e: e.dma_start(out=bqs[:], in_=bqk), inc=(LD, 16))
        p.post("gpsimd", lambda e: e.dma_start(
            out=bvrep[:], in_=bv.partition_broadcast(P)), inc=(LD, 16))
        for k in range(16):
            p.post("gpsimd", lambda e, k=k: e.dma_start(
                out=wv[:, k * DM:(k + 1) * DM],
                in_=wqkvT[k * P:(k + 1) * P, 4096:6144]), inc=(LWV, 16))
        LWV_ALL = p.val(LWV)

        for tt in range(8):
            par = tt % 2
            wvv = [("LNV", (tt - 1) * 12)] if tt >= 2 else []
            p.post("gpsimd", lambda e, tt=tt, par=par: e.dma_start(
                out=lnt[par][:], in_=fpa[tt * P:(tt + 1) * P, :]),
                inc=(LD, 16), waits=wvv)
            p.post("gpsimd", lambda e, tt=tt, par=par: e.dma_start(
                out=lba[par][:], in_=fpb[tt * P:(tt + 1) * P, :]), inc=(LD, 16))
            p.post("gpsimd", lambda e, tt=tt, par=par: e.dma_start(
                out=lbb[par][:], in_=x1own[tt * P:(tt + 1) * P, :]), inc=(LD, 16))
            ld_done = p.val(LD)
            p.post("vector", lambda e, par=par: e.tensor_add(
                out=lnt[par][:], in0=lnt[par][:], in1=lba[par][:]),
                inc=(LNV, 1), waits=[("LD", ld_done)])
            p.post("scalar", lambda e, par=par: e.activation(
                out=trash[:], in_=lnt[par][:], func=AF.Square),
                inc=(LNS, 1), waits=[("LNV", p.val(LNV))] +
                ([("LNV", (tt - 1) * 12 + 9)] if tt >= 1 else []))
            p.post("vector", lambda e: e.reduce_sum(
                stat[:, 0:1], trash[:], axis=mybir.AxisListType.X),
                inc=(LNV, 1), waits=[("LNS", p.val(LNS))])
            p.post("vector", lambda e: e.tensor_scalar(
                out=stat[:, 1:2], in0=stat[:, 0:1], scalar1=1.0 / DM, scalar2=1e-6,
                op0=ALU.mult, op1=ALU.add),
                inc=(LNV, 1), waits=[("LNV", p.val(LNV))])
            p.post("scalar", lambda e: e.activation(
                out=stat[:, 2:3], in_=stat[:, 1:2], func=AF.Ln),
                inc=(LNS, 1), waits=[("LNV", p.val(LNV))])
            p.post("vector", lambda e: e.tensor_single_scalar(
                out=stat[:, 3:4], in_=stat[:, 2:3], scalar=-0.5, op=ALU.mult),
                inc=(LNV, 1), waits=[("LNS", p.val(LNS))])
            p.post("scalar", lambda e: e.activation(
                out=stat[:, 4:5], in_=stat[:, 3:4], func=AF.Exp),
                inc=(LNS, 1), waits=[("LNV", p.val(LNV))])
            p.post("vector", lambda e, par=par: e.tensor_single_scalar(
                out=lnt[par][:], in_=lnt[par][:], scalar=stat[:, 4:5], op=ALU.mult),
                inc=(LNV, 1), waits=[("LNS", p.val(LNS))])
            p.post("vector", lambda e, par=par: e.tensor_add(
                out=lnt[par][:], in0=lnt[par][:], in1=lbb[par][:]),
                inc=(LNV, 1), waits=[("LNV", p.val(LNV))])
            p.post("vector", lambda e, par=par: e.reduce_sum(
                stat[:, 5:6], lnt[par][:], axis=mybir.AxisListType.X),
                inc=(LNV, 1), waits=[("LNV", p.val(LNV))])
            p.post("vector", lambda e: e.tensor_single_scalar(
                out=stat[:, 6:7], in_=stat[:, 5:6], scalar=-1.0 / DM, op=ALU.mult),
                inc=(LNV, 1), waits=[("LNV", p.val(LNV))])
            p.post("scalar", lambda e, par=par: e.activation(
                out=trash[:], in_=lnt[par][:], func=AF.Square, bias=stat[:, 6:7]),
                inc=(LNS, 1), waits=[("LNV", p.val(LNV))])
            p.post("vector", lambda e: e.reduce_sum(
                stat[:, 7:8], trash[:], axis=mybir.AxisListType.X),
                inc=(LNV, 1), waits=[("LNS", p.val(LNS))])
            p.post("vector", lambda e: e.tensor_scalar(
                out=stat[:, 8:9], in0=stat[:, 7:8], scalar1=1.0 / DM, scalar2=1e-5,
                op0=ALU.mult, op1=ALU.add),
                inc=(LNV, 1), waits=[("LNV", p.val(LNV))])
            p.post("scalar", lambda e: e.activation(
                out=stat[:, 9:10], in_=stat[:, 8:9], func=AF.Ln),
                inc=(LNS, 1), waits=[("LNV", p.val(LNV))])
            p.post("vector", lambda e: e.tensor_single_scalar(
                out=stat[:, 10:11], in_=stat[:, 9:10], scalar=-0.5, op=ALU.mult),
                inc=(LNV, 1), waits=[("LNS", p.val(LNS))])
            p.post("scalar", lambda e: e.activation(
                out=stat[:, 11:12], in_=stat[:, 10:11], func=AF.Exp),
                inc=(LNS, 1), waits=[("LNV", p.val(LNV))])
            p.post("vector", lambda e, par=par: e.tensor_scalar(
                out=x2t[par][:], in0=lnt[par][:], scalar1=stat[:, 6:7],
                scalar2=stat[:, 11:12], op0=ALU.add, op1=ALU.mult),
                inc=(LNV, 1), waits=[("LNS", p.val(LNS))] +
                ([("STX", (tt - 1) * 48)] if tt >= 2 else []))
            p.post("vector", lambda e, par=par: e.tensor_copy(
                out=x2b[par][:], in_=x2t[par][:]),
                inc=("XC2", 1), waits=[("LNV", p.val(LNV))])
            p.post("sync", lambda e, tt=tt, par=par: e.dma_start(
                out=x2o[tt * P:(tt + 1) * P, :], in_=x2t[par][:]),
                inc=(STX, 16), waits=[("XC2", p.val("XC2"))])
            p.post("sync", lambda e, tt=tt, par=par: e.dma_start(
                out=x2bf_dram[tt * P:(tt + 1) * P, :], in_=x2b[par][:]),
                inc=(STX, 16))
            p.post("sync", lambda e, tt=tt, par=par: e.dma_start(
                out=vo[tt * P:(tt + 1) * P, :], in_=x2b[par][:]),
                inc=(STX, 16))
        STX_A = p.val(STX)

        for k in range(16):
            p.post("sync", lambda e, k=k: e.dma_start_transpose(
                out=x2T[:, k * 1024:(k + 1) * 1024],
                in_=x2bf_dram[:, k * P:(k + 1) * P]),
                inc=(XT, 16), waits=[("STX", STX_A)] if k == 0 else ())
        XT_ALL = p.val(XT)

        for oc in range(32):
            par = oc % 2
            cw = []
            if oc >= 2:
                cw.append(("MQ", (oc - 1) * 2))
            p.post("gpsimd", lambda e, oc=oc, par=par: e.dma_start(
                out=wstr[par][:].rearrange("p (k o) -> p k o", k=16),
                in_=wqkvT.rearrange("(kc pp) o -> pp kc o", pp=P)[
                    :, :, oc * 128:(oc + 1) * 128]),
                inc=(LW, 16), waits=cw)
            for tc in range(2):
                un = oc * 2 + tc
                pq = ps_qk[tc]
                w0 = [("LW", (oc + 1) * 16), ("XT", XT_ALL)]
                if un >= 2:
                    w0.append(("SQK", un - 1))
                for k in range(16):
                    p.post("tensor", lambda e, pq=pq, par=par, k=k, tc=tc: e.matmul(
                        pq[:], wstr[par][:, k * 128:(k + 1) * 128],
                        x2T[:, k * 1024 + tc * TC: k * 1024 + (tc + 1) * TC],
                        start=(k == 0), stop=(k == 15)),
                        waits=w0 if k == 0 else (),
                        inc=(MQ, 1) if k == 15 else None)
                p.post("scalar", lambda e, pq=pq, oc=oc, un=un: e.activation(
                    out=qkst[un % 2][:], in_=pq[:], func=AF.Identity,
                    bias=bqs[:, oc:oc + 1]),
                    inc=(SQK, 1), waits=[("MQ", un + 1), ("LD", 32)] +
                    ([("STQ", (un - 1) * 16)] if un >= 2 else []))
                dst = qo if oc < 16 else ko
                ocr = oc % 16
                p.post("sync", lambda e, dst=dst, ocr=ocr, tc=tc, un=un: e.dma_start(
                    out=dst[ocr * P:(ocr + 1) * P, tc * TC:(tc + 1) * TC],
                    in_=qkst[un % 2][:]),
                    inc=(STQ, 16), waits=[("SQK", un + 1)])

        for tt in range(8):
            for od in range(4):
                un = tt * 4 + od
                pv = ps_v[un % 2]
                w0 = [("LWV", LWV_ALL), ("XT", XT_ALL)]
                if un >= 2:
                    w0.append(("VV", un - 1))
                for k in range(16):
                    p.post("tensor", lambda e, pv=pv, k=k, tt=tt, od=od: e.matmul(
                        pv[:], x2T[:, k * 1024 + tt * 128: k * 1024 + (tt + 1) * 128],
                        wv[:, k * DM + od * TC: k * DM + (od + 1) * TC],
                        start=(k == 0), stop=(k == 15)),
                        waits=w0 if k == 0 else (),
                        inc=(MV, 1) if k == 15 else None)
                p.post("vector", lambda e, pv=pv, un=un, od=od: e.tensor_add(
                    out=vst[un % 2][:].bitcast(BF16), in0=pv[:],
                    in1=bvrep[:, od * TC:(od + 1) * TC]),
                    inc=(VV, 1), waits=[("MV", un + 1)] +
                    ([("STV", (un - 1) * 16)] if un >= 2 else []))
                p.post("sync", lambda e, tt=tt, od=od, un=un: e.dma_start(
                    out=vo[tt * P:(tt + 1) * P, od * TC:(od + 1) * TC],
                    in_=vst[un % 2][:]),
                    inc=(STV, 16), waits=[("VV", un + 1)])
        p.post("sync", lambda e: None,
               waits=[("STQ", p.val(STQ)), ("STV", p.val(STV)), ("STX", p.val(STX))])
        p.build(ctx)
    return nc


def host_inputs_launch5(inp, c, l3outs, l4outs):
    b, s = c // 2, c % 2
    even, odd = 2 * (c // 2), 2 * (c // 2) + 1
    tsl = slice(s * 1024, (s + 1) * 1024)
    asc = np.ascontiguousarray
    return dict(
        fpa=asc(l4outs[even]["o_fp"][tsl]), fpb=asc(l4outs[odd]["o_fp"][tsl]),
        x1own=l3outs[c]["x1o"],
        wqkvT=asc(inp["L1_attn_Wqkv"].T),
        bqk=percol(inp["L1_attn_b_qkv"][:4096]),
        bv=asc(inp["L1_attn_b_qkv"][None, 4096:]),
    )


def build_launch6(h_lim=8, qt_lim=16, taps=False):
    nc = bass.Bass("TRN2", target_bir_lowering=False, debug=False)
    qa = nc.dram_tensor("qa", [1024, 1024], BF16, kind="ExternalInput").ap()
    qb = nc.dram_tensor("qb", [1024, 1024], BF16, kind="ExternalInput").ap()
    ka = nc.dram_tensor("ka", [1024, 1024], BF16, kind="ExternalInput").ap()
    kb = nc.dram_tensor("kb", [1024, 1024], BF16, kind="ExternalInput").ap()
    va = nc.dram_tensor("va", [1024, 1024], BF16, kind="ExternalInput").ap()
    vb = nc.dram_tensor("vb", [1024, 1024], BF16, kind="ExternalInput").ap()
    alibi = nc.dram_tensor("alibi", [8, T], F32, kind="ExternalInput").ap()
    dmask = nc.dram_tensor("dmask", [P, P], F32, kind="ExternalInput").ap()
    ident = nc.dram_tensor("ident", [P, P], BF16, kind="ExternalInput").ap()
    o_ctx = nc.dram_tensor("o_ctx", [1024, T], BF16, kind="ExternalOutput").ap()

    SCALE = float(1.0 / np.sqrt(128.0))

    with ExitStack() as ctx:
        kh = [sbuf(nc, ctx, f"kh{i}", [P, T], BF16) for i in range(2)]
        qh = [sbuf(nc, ctx, f"qh{i}", [P, T], BF16) for i in range(2)]
        vh = [sbuf(nc, ctx, f"vh{i}", [P, T], BF16) for i in range(2)]
        arep = [sbuf(nc, ctx, f"arep{i}", [P, T], F32) for i in range(2)]
        guard_junk = sbuf(nc, ctx, "guard_junk", [P, T], F32)
        ssb = [sbuf(nc, ctx, f"ssb{i}", [P, T], F32) for i in range(2)]
        psb = [sbuf(nc, ctx, f"psb{i}", [P, T], BF16) for i in range(2)]
        dmsk = sbuf(nc, ctx, "dmsk", [P, P], F32)
        idn = sbuf(nc, ctx, "idn", [P, P], BF16)
        stat = sbuf(nc, ctx, "stat", [P, 8], F32)
        settle = sbuf(nc, ctx, "settle", [P, 256], F32)
        pts = sbuf(nc, ctx, "pts", [P, 16 * P], BF16)
        ctxs = [sbuf(nc, ctx, f"ctxs{i}", [P, P], BF16) for i in range(2)]
        ctxT = [sbuf(nc, ctx, f"ctxT{i}", [P, P], BF16) for i in range(2)]
        ps_s = [psum(nc, ctx, f"ps_s{i}", [P, TC]) for i in range(2)]
        ps_pt = [psum(nc, ctx, f"ps_pt{i}", [P, P], BF16) for i in range(2)]
        ps_c = [psum(nc, ctx, f"ps_c{i}", [P, P]) for i in range(2)]
        ps_ct = [psum(nc, ctx, f"ps_ct{i}", [P, P], BF16) for i in range(2)]

        p = Prog(nc)
        LD0 = "LD0"; LKV = "LKV"; MS = "MS"; SEV = "SEV"; RMX = "RMX"
        EXP = "EXP"; RED = "RED"; TRP = "TRP"; PEV = "PEV"; MCT = "MCT"
        CEV = "CEV"; CTR = "CTR"; CTE = "CTE"; STC = "STC"

        p.post("gpsimd", lambda e: e.dma_start(out=dmsk[:], in_=dmask), inc=(LD0, 16))
        p.post("gpsimd", lambda e: e.dma_start(out=idn[:], in_=ident), inc=(LD0, 16))
        LD0_ALL = p.val(LD0)

        last_use = {}

        def emit_ctx(h, qt, hp, qp, ex_gate, ex_own, rc_own):
            nk = qt + 1
            pe_marks = []
            for kb_ in range(nk):
                bp = kb_ % 2
                kpt = last_use.get(("ps_pt", bp))
                tr = p.post("tensor", lambda e, qp=qp, bp=bp, kb_=kb_: e.transpose(
                    ps_pt[bp][:], psb[qp][:, kb_ * P:(kb_ + 1) * P], idn[:]),
                    inc=("TRP", 1), waits=[("EXP", ex_gate), ("LD0", LD0_ALL)] +
                    ([kpt] if kpt else []))
                kpe = last_use.get(("pts", kb_))
                pe = p.post("vector", lambda e, bp=bp, kb_=kb_: e.tensor_copy(
                    out=pts[:, kb_ * P:(kb_ + 1) * P], in_=ps_pt[bp][:]),
                    inc=("PEV", 1), waits=[("TRP", tr)] + ([kpe] if kpe else []))
                last_use[("ps_pt", bp)] = ("PEV", pe)
                pe_marks.append(pe)
            p.post("vector", lambda e: e.tensor_copy(
                out=settle[:, 0:128], in_=dmsk[:]), inc=("PEV", 1))
            for kb_ in range(nk):
                mc = p.post("tensor", lambda e, hp=hp, qp=qp, kb_=kb_, nk=nk: e.matmul(
                    ps_c[qp][:], pts[:, kb_ * P:(kb_ + 1) * P],
                    vh[hp][:, kb_ * P:(kb_ + 1) * P],
                    start=(kb_ == 0), stop=(kb_ == nk - 1)),
                    inc=("MCT", 1), waits=[("PEV", pe_marks[kb_] + 1)])
                last_use[("pts", kb_)] = ("MCT", mc)
            kce = last_use.get(("ctxs", qp))
            ce = p.post("vector", lambda e, qp=qp: e.tensor_single_scalar(
                out=ctxs[qp][:], in_=ps_c[qp][:],
                scalar=stat[:, qp * 4 + 3: qp * 4 + 4], op=ALU.mult),
                inc=("CEV", 1), waits=[("MCT", p.val("MCT")), ("RED", rc_own)] +
                ([kce] if kce else []))
            ce = p.post("vector", lambda e: e.tensor_copy(
                out=settle[:, 0:128], in_=dmsk[:]), inc=("CEV", 1),
                waits=[("CEV", ce)])
            kct = last_use.get(("ps_ct", qp))
            ct = p.post("tensor", lambda e, qp=qp: e.transpose(
                ps_ct[qp][:], ctxs[qp][:], idn[:]),
                inc=("CTR", 1), waits=[("CEV", ce)] + ([kct] if kct else []))
            kte = last_use.get(("ctxT", qp))
            te = p.post("vector", lambda e, qp=qp: e.tensor_copy(
                out=ctxT[qp][:], in_=ps_ct[qp][:]),
                inc=("CTE", 1), waits=[("CTR", ct)] + ([kte] if kte else []))
            last_use[("ps_ct", qp)] = ("CTE", te)
            te = p.post("vector", lambda e: e.tensor_copy(
                out=settle[:, 0:128], in_=dmsk[:]), inc=("CTE", 1),
                waits=[("CTE", te)])
            st = p.post("sync", lambda e, h=h, qt=qt, qp=qp: e.dma_start(
                out=o_ctx[h * P:(h + 1) * P, qt * P:(qt + 1) * P], in_=ctxT[qp][:]),
                inc=("STC", 16), waits=[("CTE", te)])
            last_use[("ctxT", qp)] = ("STC", st)
            last_use[("ctxs", qp)] = ("CTR", ct)
            last_use[("ssb", qp)] = ("EXP", ex_own)
            last_use[("psb", qp)] = ("MCT", p.val("MCT"))

        pending = None
        unit_ctr = [0]
        for h in range(h_lim):
            hp = h % 2
            wv = [("STC", p.val(STC))] if h >= 2 else []
            p.post("gpsimd", lambda e, h=h, hp=hp: e.dma_start(
                out=kh[hp][:, 0:1024], in_=ka[h * P:(h + 1) * P, :]),
                inc=(LKV, 16), waits=wv)
            p.post("gpsimd", lambda e, h=h, hp=hp: e.dma_start(
                out=kh[hp][:, 1024:2048], in_=kb[h * P:(h + 1) * P, :]), inc=(LKV, 16))
            p.post("gpsimd", lambda e, h=h, hp=hp: e.dma_start(
                out=qh[hp][:, 0:1024], in_=qa[h * P:(h + 1) * P, :]), inc=(LKV, 16))
            p.post("gpsimd", lambda e, h=h, hp=hp: e.dma_start(
                out=qh[hp][:, 1024:2048], in_=qb[h * P:(h + 1) * P, :]), inc=(LKV, 16))
            p.post("gpsimd", lambda e, h=h, hp=hp: e.dma_start(
                out=vh[hp][:, 0:1024].rearrange("p (tk c) -> p tk c", tk=8),
                in_=va.rearrange("(tk p) c -> p tk c", p=P)[:, :, h * P:(h + 1) * P]),
                inc=(LKV, 16))
            p.post("gpsimd", lambda e, h=h, hp=hp: e.dma_start(
                out=vh[hp][:].rearrange("p (tk c) -> p tk c", tk=16)[:, 8:16, :],
                in_=vb.rearrange("(tk p) c -> p tk c", p=P)[:, :, h * P:(h + 1) * P]),
                inc=(LKV, 16))
            p.post("gpsimd", lambda e, h=h, hp=hp: e.dma_start(
                out=arep[hp][:], in_=alibi[h:h + 1, :].partition_broadcast(P)),
                inc=(LKV, 16))
            lkv_done = p.val(LKV)

            qt_list = ([0] if h == 0 else []) + list(range(qt_lim))
            for qi, qt in enumerate(qt_list):
                unit_ctr[0] += 1
                qp = unit_ctr[0] % 2
                nk = qt + 1
                nch = (nk * P + TC - 1) // TC
                sev_marks = []
                for kc in range(nch):
                    n = min(TC, nk * P - kc * TC)
                    w0 = [("LKV", lkv_done)]
                    k_ = last_use.get(("ps_s", qp))
                    if k_ is not None:
                        w0.append(k_)
                    mcount = p.post("tensor", lambda e, hp=hp, qp=qp, qt=qt, kc=kc, n=n: e.matmul(
                        ps_s[qp][:, 0:n], qh[hp][:, qt * P:(qt + 1) * P],
                        kh[hp][:, kc * TC: kc * TC + n], start=True, stop=True),
                        inc=(MS, 1), waits=w0)
                    k2 = last_use.get(("ssb", qp))
                    sev = p.post("vector", lambda e, hp=hp, qp=qp, kc=kc, n=n: e.scalar_tensor_tensor(
                        out=ssb[qp][:, kc * TC: kc * TC + n], in0=ps_s[qp][:, 0:n],
                        scalar=SCALE, in1=arep[hp][:, kc * TC: kc * TC + n],
                        op0=ALU.mult, op1=ALU.add),
                        inc=(SEV, 1), waits=[("MS", mcount)] +
                        ([k2] if (k2 is not None and kc == 0) else []))
                    sev_marks.append(sev)
                    last_use[("ps_s", qp)] = ("SEV", sev)
                mk = p.post("vector", lambda e, qp=qp, qt=qt: e.tensor_add(
                    out=ssb[qp][:, qt * P:(qt + 1) * P],
                    in0=ssb[qp][:, qt * P:(qt + 1) * P], in1=dmsk[:]),
                    inc=(SEV, 1), waits=[("SEV", sev_marks[-1]), ("LD0", LD0_ALL)])
                mk = p.post("vector", lambda e: e.tensor_copy(
                    out=settle[:, 0:128], in_=dmsk[:]),
                    inc=(SEV, 1), waits=[("SEV", mk)])
                rm = p.post("vector", lambda e, qp=qp, nk=nk: e.reduce_max(
                    stat[:, qp * 4: qp * 4 + 1], ssb[qp][:, 0:nk * P],
                    axis=mybir.AxisListType.X),
                    inc=(RMX, 1), waits=[("SEV", mk)])
                rm2 = p.post("vector", lambda e, qp=qp: e.tensor_single_scalar(
                    out=stat[:, qp * 4 + 1: qp * 4 + 2], in_=stat[:, qp * 4: qp * 4 + 1],
                    scalar=-1.0, op=ALU.mult),
                    inc=(RMX, 1), waits=[("RMX", rm)])
                rm2 = p.post("vector", lambda e, qp=qp: e.tensor_copy(
                    out=settle[:], in_=ssb[qp][:, 0:256]),
                    inc=(RMX, 1), waits=[("RMX", rm2)])
                k3 = last_use.get(("psb", qp))
                ex = p.post("scalar", lambda e, qp=qp, nk=nk: e.activation(
                    out=psb[qp][:, 0:nk * P], in_=ssb[qp][:, 0:nk * P], func=AF.Exp,
                    bias=stat[:, qp * 4 + 1: qp * 4 + 2]),
                    inc=(EXP, 1), waits=[("RMX", rm2)] + ([k3] if k3 else []))
                rd = p.post("vector", lambda e, qp=qp, nk=nk: e.reduce_sum(
                    stat[:, qp * 4 + 2: qp * 4 + 3], psb[qp][:, 0:nk * P],
                    axis=mybir.AxisListType.X),
                    inc=(RED, 1), waits=[("EXP", ex)])
                rd = p.post("vector", lambda e: e.tensor_copy(
                    out=settle[:, 0:128], in_=dmsk[:]),
                    inc=(RED, 1), waits=[("RED", rd)])
                rc = p.post("vector", lambda e, qp=qp: e.reciprocal(
                    out=stat[:, qp * 4 + 3: qp * 4 + 4], in_=stat[:, qp * 4 + 2: qp * 4 + 3]),
                    inc=(RED, 1), waits=[("RED", rd)])
                if pending is not None:
                    emit_ctx(*pending[:4], ex_gate=ex, ex_own=pending[4], rc_own=pending[5])
                pending = (h, qt, hp, qp, ex, rc)
        if pending is not None:
            fex = p.post("scalar", lambda e: e.activation(
                out=settle[:, 0:256], in_=ssb[pending[3]][:, 0:256], func=AF.Identity),
                inc=("EXP", 1), waits=[("EXP", pending[4])])
            emit_ctx(*pending[:4], ex_gate=fex, ex_own=pending[4], rc_own=pending[5])
        p.post("sync", lambda e: None, waits=[("STC", p.val(STC))])
        p.build(ctx)
    return nc


def host_inputs_launch6(inp, c, l5outs):
    b, s = c // 2, c % 2
    even, odd = 2 * (c // 2), 2 * (c // 2) + 1
    asc = np.ascontiguousarray
    rsl = slice(s * 1024, (s + 1) * 1024)
    H = 16
    slopes = (1.0 / 2.0 ** (np.arange(1, H + 1, dtype=np.float32) * 0.5)).astype(np.float32)
    bias = (np.arange(1 - T, 1, dtype=np.float32)[None, :] * slopes[:, None]).astype(np.float32)
    dmask = np.where(np.tril(np.ones((P, P), bool)), 0.0, -1e9).astype(np.float32)
    return dict(
        qa=asc(l5outs[even]["qo"][rsl]), qb=asc(l5outs[odd]["qo"][rsl]),
        ka=asc(l5outs[even]["ko"][rsl]), kb=asc(l5outs[odd]["ko"][rsl]),
        va=asc(l5outs[even]["vo"][:, rsl]), vb=asc(l5outs[odd]["vo"][:, rsl]),
        alibi=asc(bias[s * 8:(s + 1) * 8]), dmask=dmask,
        ident=np.eye(P, dtype=np.float32).astype(BF),
    )


def build_launch7():
    nc = bass.Bass("TRN2", target_bir_lowering=False, debug=False)
    ctxa = nc.dram_tensor("ctxa", [1024, 1024], BF16, kind="ExternalInput").ap()
    ctxb = nc.dram_tensor("ctxb", [1024, 1024], BF16, kind="ExternalInput").ap()
    woT1 = nc.dram_tensor("woT1", [DM, DM], F32, kind="ExternalInput").ap()
    bo1 = nc.dram_tensor("bo1", [1, DM], F32, kind="ExternalInput").ap()
    x2own = nc.dram_tensor("x2own", [1024, DM], F32, kind="ExternalInput").ap()
    x3o = nc.dram_tensor("x3o", [1024, DM], F32, kind="ExternalOutput").ap()

    with ExitStack() as ctx:
        ctxT = sbuf(nc, ctx, "ctxT", [P, 16 * 1024], BF16)
        wo = sbuf(nc, ctx, "wo", [P, 16 * DM], BF16)
        borep = sbuf(nc, ctx, "borep", [P, DM], F32)
        x2s = [sbuf(nc, ctx, f"x2s{i}", [P, DM], F32) for i in range(2)]
        acc = [sbuf(nc, ctx, f"acc{i}", [P, DM], F32) for i in range(2)]
        x3t = [sbuf(nc, ctx, f"x3t{i}", [P, DM], F32) for i in range(2)]
        trash = sbuf(nc, ctx, "trash", [P, DM], F32)
        stat = sbuf(nc, ctx, "stat", [P, 8], F32)
        ps_o = [psum(nc, ctx, f"ps_o{i}", [P, TC]) for i in range(2)]

        p = Prog(nc)
        LD = "LD"; LDW = "LDW"; MO = "MO"; EVO = "EVO"
        LNV = "LNV"; LNS = "LNS"; STX = "STX"

        p.post("gpsimd", lambda e: e.dma_start(
            out=borep[:], in_=bo1.partition_broadcast(P)), inc=(LD, 16))
        for k in range(16):
            src = ctxa if k < 8 else ctxb
            kr = k % 8
            p.post("gpsimd", lambda e, src=src, kr=kr, k=k: e.dma_start(
                out=ctxT[:, k * 1024:(k + 1) * 1024],
                in_=src[kr * P:(kr + 1) * P, :]),
                inc=(LD, 16))
        for k in range(16):
            p.post("gpsimd", lambda e, k=k: e.dma_start(
                out=wo[:, k * DM:(k + 1) * DM], in_=woT1[k * P:(k + 1) * P, :]),
                inc=(LDW, 16))
        LD_ALL = p.val(LD)
        LDW_ALL = p.val(LDW)

        for tt in range(8):
            par = tt % 2
            for od in range(4):
                un = tt * 4 + od
                po = ps_o[un % 2]
                w0 = [("LD", LD_ALL), ("LDW", LDW_ALL)]
                if un >= 2:
                    w0.append(("EVO", un - 1))
                for k in range(16):
                    p.post("tensor", lambda e, po=po, k=k, tt=tt, od=od: e.matmul(
                        po[:], ctxT[:, k * 1024 + tt * 128: k * 1024 + (tt + 1) * 128],
                        wo[:, k * DM + od * TC: k * DM + (od + 1) * TC],
                        start=(k == 0), stop=(k == 15)),
                        waits=w0 if k == 0 else (),
                        inc=(MO, 1) if k == 15 else None)
                p.post("vector", lambda e, po=po, par=par, od=od, un=un: e.tensor_add(
                    out=acc[par][:, od * TC:(od + 1) * TC], in0=po[:],
                    in1=borep[:, od * TC:(od + 1) * TC]),
                    inc=(EVO, 1), waits=[("MO", un + 1)] +
                    ([("LNV", (tt - 2) * 8 + 8)] if (tt >= 2 and od == 0) else []))
            p.post("gpsimd", lambda e, tt=tt, par=par: e.dma_start(
                out=x2s[par][:], in_=x2own[tt * P:(tt + 1) * P, :]),
                inc=(LD, 16), waits=[("LNV", (tt - 2) * 8 + 1)] if tt >= 2 else [])
            ld2 = p.val(LD)
            p.post("vector", lambda e, par=par: e.tensor_add(
                out=acc[par][:], in0=acc[par][:], in1=x2s[par][:]),
                inc=(LNV, 1), waits=[("EVO", (tt + 1) * 4), ("LD", ld2)])
            p.post("vector", lambda e, par=par: e.reduce_sum(
                stat[:, 0:1], acc[par][:], axis=mybir.AxisListType.X),
                inc=(LNV, 1), waits=[("LNV", p.val(LNV))])
            p.post("vector", lambda e: e.tensor_single_scalar(
                out=stat[:, 1:2], in_=stat[:, 0:1], scalar=-1.0 / DM, op=ALU.mult),
                inc=(LNV, 1), waits=[("LNV", p.val(LNV))])
            p.post("vector", lambda e: e.tensor_copy(
                out=trash[:, 0:256], in_=borep[:, 0:256]),
                inc=(LNV, 1), waits=[("LNV", p.val(LNV))])
            p.post("scalar", lambda e, par=par: e.activation(
                out=trash[:], in_=acc[par][:], func=AF.Square, bias=stat[:, 1:2]),
                inc=(LNS, 1), waits=[("LNV", p.val(LNV))] +
                ([("LNV", (tt - 1) * 8 + 5)] if tt >= 1 else []))
            p.post("vector", lambda e: e.reduce_sum(
                stat[:, 2:3], trash[:], axis=mybir.AxisListType.X),
                inc=(LNV, 1), waits=[("LNS", p.val(LNS))])
            p.post("vector", lambda e: e.tensor_scalar(
                out=stat[:, 3:4], in0=stat[:, 2:3], scalar1=1.0 / DM, scalar2=1e-5,
                op0=ALU.mult, op1=ALU.add),
                inc=(LNV, 1), waits=[("LNV", p.val(LNV))])
            p.post("vector", lambda e: e.tensor_copy(
                out=trash[:, 0:256], in_=borep[:, 0:256]),
                inc=(LNV, 1), waits=[("LNV", p.val(LNV))])
            p.post("scalar", lambda e: e.activation(
                out=stat[:, 4:5], in_=stat[:, 3:4], func=AF.Ln),
                inc=(LNS, 1), waits=[("LNV", p.val(LNV))])
            p.post("vector", lambda e: e.tensor_single_scalar(
                out=stat[:, 5:6], in_=stat[:, 4:5], scalar=-0.5, op=ALU.mult),
                inc=(LNV, 1), waits=[("LNS", p.val(LNS))])
            p.post("scalar", lambda e: e.activation(
                out=stat[:, 6:7], in_=stat[:, 5:6], func=AF.Exp),
                inc=(LNS, 1), waits=[("LNV", p.val(LNV))])
            p.post("vector", lambda e, par=par: e.tensor_scalar(
                out=x3t[par][:], in0=acc[par][:], scalar1=stat[:, 1:2],
                scalar2=stat[:, 6:7], op0=ALU.add, op1=ALU.mult),
                inc=("XF", 1), waits=[("LNS", p.val(LNS))] +
                ([("STX", (tt - 2) * 16 + 16)] if tt >= 2 else []))
            p.post("sync", lambda e, tt=tt, par=par: e.dma_start(
                out=x3o[tt * P:(tt + 1) * P, :], in_=x3t[par][:]),
                inc=(STX, 16), waits=[("XF", p.val("XF"))])
        p.post("sync", lambda e: None, waits=[("STX", p.val(STX))])
        p.build(ctx)
    return nc


def host_inputs_launch7(inp, c, l5outs, l6outs):
    b, s = c // 2, c % 2
    even, odd = 2 * (c // 2), 2 * (c // 2) + 1
    asc = np.ascontiguousarray
    tsl = slice(s * 1024, (s + 1) * 1024)
    return dict(
        ctxa=asc(l6outs[even]["o_ctx"][:, tsl]),
        ctxb=asc(l6outs[odd]["o_ctx"][:, tsl]),
        woT1=asc(inp["L1_attn_Wout"].T),
        bo1=asc(inp["L1_attn_b_out"][None, :]),
        x2own=l5outs[c]["x2o"],
    )


def build_launch8(phases="SGTHD"):
    nc = bass.Bass("TRN2", target_bir_lowering=False, debug=False)
    x3a = nc.dram_tensor("x3a", [1024, DM], F32, kind="ExternalInput").ap()
    x3b = nc.dram_tensor("x3b", [1024, DM], F32, kind="ExternalInput").ap()
    wgp = nc.dram_tensor("wgp", [DM, 8], F32, kind="ExternalInput").ap()
    bgp = nc.dram_tensor("bgp", [1, 8], F32, kind="ExternalInput").ap()
    weg = nc.dram_tensor("weg", [3 * DM, 1024], F32, kind="ExternalInput").ap()
    weu = nc.dram_tensor("weu", [3 * DM, 1024], F32, kind="ExternalInput").ap()
    wed = nc.dram_tensor("wed", [3 * 1024, DM], F32, kind="ExternalInput").ap()
    wsu = nc.dram_tensor("wsu", [DM, 1024], F32, kind="ExternalInput").ap()
    wsd = nc.dram_tensor("wsd", [1024, DM], F32, kind="ExternalInput").ap()
    bge = nc.dram_tensor("bge", [P, 24], F32, kind="ExternalInput").ap()
    bue = nc.dram_tensor("bue", [P, 24], F32, kind="ExternalInput").ap()
    bsu = nc.dram_tensor("bsu", [P, 8], F32, kind="ExternalInput").ap()
    bd3 = nc.dram_tensor("bd3", [3, DM], F32, kind="ExternalInput").ap()
    bsd = nc.dram_tensor("bsd", [1, DM], F32, kind="ExternalInput").ap()
    ones = nc.dram_tensor("ones", [1, T], BF16, kind="ExternalInput").ap()
    idin = nc.dram_tensor("idin", [P, P], BF16, kind="ExternalInput").ap()
    o_mp = nc.dram_tensor("o_mp", [DM, T], BF16, kind="ExternalOutput").ap()
    x3hi_dram = nc.dram_tensor("x3hi_dram", [T, DM], BF16, kind="Internal").ap()
    x3lo_dram = nc.dram_tensor("x3lo_dram", [T, DM], BF16, kind="Internal").ap()
    wgu_dram = nc.dram_tensor("wgu_dram", [7 * DM, 1024], BF16, kind="Internal").ap()
    wdb_dram = nc.dram_tensor("wdb_dram", [4096, DM], BF16, kind="Internal").ap()
    wt_dram = nc.dram_tensor("wt_dram", [8, T], BF16, kind="Internal").ap()

    with ExitStack() as ctx:
        bigA = sbuf(nc, ctx, "bigA", [P, 32 * 1024], BF16)
        bigB = sbuf(nc, ctx, "bigB", [P, 16 * T], BF16)
        stg = [sbuf(nc, ctx, f"stg_{i}", [P, DM], F32) for i in range(2)]
        stg1 = stg[0]
        stgb = [sbuf(nc, ctx, f"stgb_{i}", [P, DM], BF16) for i in range(2)]
        stgb1 = stgb[0]
        wstr = [sbuf(nc, ctx, f"wstr{i}", [P, 16 * 128], BF16) for i in range(2)]
        wstr2 = [sbuf(nc, ctx, f"wstr2_{i}", [P, 16 * 128], BF16) for i in range(2)]
        wgs = sbuf(nc, ctx, "wgs", [P, 128], BF16)
        wgl = sbuf(nc, ctx, "wgl", [P, 128], BF16)
        wgf = sbuf(nc, ctx, "wgf", [P, 128], F32)
        gate = sbuf(nc, ctx, "gate", [P, 16 * 8], F32)
        rk = sbuf(nc, ctx, "rk", [P, 8], F32)
        sel = sbuf(nc, ctx, "sel", [P, 8], F32)
        esel = sbuf(nc, ctx, "esel", [P, 8], F32)
        wsb = sbuf(nc, ctx, "wsb", [P, 8], F32)
        stat8 = sbuf(nc, ctx, "stat8", [P, 8], F32)
        settle8 = sbuf(nc, ctx, "settle8", [P, 256], F32)
        wbpad = sbuf(nc, ctx, "wbpad", [P, P], BF16)
        wts = sbuf(nc, ctx, "wts", [8, P], BF16)
        wrep1 = sbuf(nc, ctx, "wrep1", [P, T], BF16)
        w3 = sbuf(nc, ctx, "w3", [3, T], BF16)
        bd3s = sbuf(nc, ctx, "bd3s", [3, DM], BF16)
        bsds = sbuf(nc, ctx, "bsds", [1, DM], BF16)
        oness = sbuf(nc, ctx, "oness", [1, T], BF16)
        bges = sbuf(nc, ctx, "bges", [P, 24], F32)
        bues = sbuf(nc, ctx, "bues", [P, 24], F32)
        bsus = sbuf(nc, ctx, "bsus", [P, 8], F32)
        bgrep = sbuf(nc, ctx, "bgrep", [P, 8], F32)
        hgst = [sbuf(nc, ctx, f"hgst{i}", [P, TC], BF16) for i in range(2)]
        hust = [sbuf(nc, ctx, f"hust{i}", [P, TC], BF16) for i in range(2)]
        mpst = [sbuf(nc, ctx, f"mpst{i}", [P, TC], BF16) for i in range(2)]
        idn8 = sbuf(nc, ctx, "idn8", [P, P], BF16)
        xls = [sbuf(nc, ctx, f"xls{i}", [P, TC], BF16) for i in range(2)]
        xhs = [sbuf(nc, ctx, f"xhs{i}", [P, TC], BF16) for i in range(2)]
        ps_ga = [psum(nc, ctx, f"ps_ga{i}", [P, 8]) for i in range(4)]
        ps_g8 = [psum(nc, ctx, f"ps_g8{i}", [P, TC]) for i in range(2)]
        ps_w = psum(nc, ctx, "ps_w", [P, P], BF16)
        ps_d8 = psum(nc, ctx, "ps_d8", [P, TC])

        p = Prog(nc)
        LDM = "LDM"; XS = "XS"; XC = "XC"; XD = "XD"; XT = "XT"; XLT = "XLT"
        MGA = "MGA"; GEV = "GEV"; TKV = "TKV"; TKS = "TKS"
        WTR = "WTR"; WEV = "WEV"; WDMA = "WDMA"; WBC = "WBC"
        WCX = "WCX"; WCD = "WCD"; WCL = "WCL"; WRL = "WRL"
        LWS = "LWS"; MGU = "MGU"; SGU = "SGU"; HVV = "HVV"
        LDD = "LDD"; MDN = "MDN"; DEV = "DEV"; STM = "STM"

        for dst, src in ((bges, bge), (bues, bue), (bsus, bsu), (bd3s, bd3),
                         (bsds, bsd), (oness, ones), (idn8, idin)):
            p.post("gpsimd", lambda e, dst=dst, src=src: e.dma_start(out=dst[:], in_=src),
                   inc=(LDM, 16))
        p.post("gpsimd", lambda e: e.dma_start(
            out=bgrep[:], in_=bgp.partition_broadcast(P)), inc=(LDM, 16))
        p.post("gpsimd", lambda e: e.dma_start(
            out=wgs[:].rearrange("p (k o) -> p k o", k=16),
            in_=wgp.rearrange("(k p) o -> p k o", p=P)), inc=(LDM, 16))
        p.post("gpsimd", lambda e: e.dma_start(
            out=wgf[:].rearrange("p (k o) -> p k o", k=16),
            in_=wgp.rearrange("(k p) o -> p k o", p=P)), inc=(LDM, 16))
        LDM_ALL = p.val(LDM)
        wlm = p.post("vector", lambda e: e.tensor_sub(
            out=wgl[:], in0=wgf[:], in1=wgs[:]), inc=("WGL", 1),
            waits=[("LDM", LDM_ALL)])
        p.post("vector", lambda e: e.memset(wbpad[:], 0.0), inc=("WBP", 1))

        lo_marks = {}
        for i in range(16):
            src = x3a if i < 8 else x3b
            row = (i % 8) * P
            pr = i % 2
            w0 = [("WGL", wlm)] if i < 2 else [("XD", lo_marks[i - 2])]
            l1 = p.post("gpsimd", lambda e, src=src, row=row, pr=pr: e.dma_start(
                out=stg[pr][:], in_=src[row:row + P, :]), inc=(XS, 16), waits=w0)
            c1 = p.post("vector", lambda e, pr=pr: e.tensor_copy(
                out=stgb[pr][:], in_=stg[pr][:]), inc=(XC, 1), waits=[("XS", l1)])
            d1 = p.post("gpsimd", lambda e, i=i, pr=pr: e.dma_start(
                out=x3hi_dram[i * P:(i + 1) * P, :], in_=stgb[pr][:]),
                inc=(XD, 16), waits=[("XC", c1)])
            c2 = p.post("vector", lambda e, pr=pr: e.tensor_sub(
                out=stgb[pr][:], in0=stg[pr][:], in1=stgb[pr][:]),
                inc=(XC, 1), waits=[("XD", d1)])
            d2 = p.post("gpsimd", lambda e, i=i, pr=pr: e.dma_start(
                out=x3lo_dram[i * P:(i + 1) * P, :], in_=stgb[pr][:]),
                inc=(XD, 16), waits=[("XC", c2)])
            lo_marks[i] = d2
        XD_ALL = p.val(XD)

        cast_ctr = [0]
        cast_out_marks = {}
        def cast_rows(src_ap, n_tiles, dst_ap, dst_row0):
            for i in range(n_tiles):
                ci = cast_ctr[0]; cast_ctr[0] += 1
                pr = ci % 2
                if ci < 2:
                    w0 = [("XD", XD_ALL)]
                else:
                    w0 = [("WCD", cast_out_marks[ci - 2])]
                l1 = p.post("gpsimd", lambda e, i=i, src_ap=src_ap, pr=pr: e.dma_start(
                    out=stg[pr][:, 0:src_ap.shape[1]], in_=src_ap[i * P:(i + 1) * P, :]),
                    inc=(WCL, 16), waits=w0)
                c1 = p.post("vector", lambda e, src_ap=src_ap, pr=pr: e.tensor_copy(
                    out=stgb[pr][:, 0:src_ap.shape[1]], in_=stg[pr][:, 0:src_ap.shape[1]]),
                    inc=(WCX, 1), waits=[("WCL", l1)])
                dm = p.post("gpsimd", lambda e, i=i, dst_ap=dst_ap, dst_row0=dst_row0,
                       src_ap=src_ap, pr=pr: e.dma_start(
                    out=dst_ap[dst_row0 + i * P: dst_row0 + (i + 1) * P, :],
                    in_=stgb[pr][:, 0:src_ap.shape[1]]),
                    inc=(WCD, 16), waits=[("WCX", c1)])
                cast_out_marks[ci] = dm
        cast_rows(weg, 48, wgu_dram, 0)
        cast_rows(weu, 48, wgu_dram, 48 * P)
        cast_rows(wsu, 16, wgu_dram, 96 * P)
        cast_rows(wed, 24, wdb_dram, 0)
        cast_rows(wsd, 8, wdb_dram, 24 * P)
        WCD_ALL = p.val(WCD)

        if "G" not in phases:
            p.post("sync", lambda e: None, waits=[("WCD", WCD_ALL), ("XD", XD_ALL)])
            p.build(ctx)
            return nc
        kmm_done = {}
        for tq in range(4):
            for k in range(16):
                kp = k % 2
                wv = [("XD", XD_ALL), ("WCD", WCD_ALL)]
                if tq > 0 or k >= 2:
                    pk = (tq * 16 + k) - 2
                    wv.append(("MGA", kmm_done[pk]))
                p.post("sync", lambda e, k=k, kp=kp, tq=tq: e.dma_start_transpose(
                    out=xhs[kp][:], in_=x3hi_dram[tq * TC:(tq + 1) * TC, k * P:(k + 1) * P]),
                    inc=(XLT, 16), waits=wv)
                p.post("sync", lambda e, k=k, kp=kp, tq=tq: e.dma_start_transpose(
                    out=xls[kp][:], in_=x3lo_dram[tq * TC:(tq + 1) * TC, k * P:(k + 1) * P]),
                    inc=(XLT, 16))
                xlt = p.val(XLT)
                for tt in range(4):
                    pg = ps_ga[tt]
                    for ps_ in range(3):
                        xbuf = xhs[kp] if ps_ < 2 else xls[kp]
                        wbuf = wgs if ps_ != 1 else wgl
                        p.post("tensor", lambda e, pg=pg, xbuf=xbuf, wbuf=wbuf, k=k, tt=tt, ps_=ps_:
                               e.matmul(pg[:], xbuf[:, tt * P:(tt + 1) * P],
                                        wbuf[:, k * 8:(k + 1) * 8],
                                        start=(k == 0 and ps_ == 0),
                                        stop=(k == 15 and ps_ == 2),
                                        skip_group_check=True),
                               waits=[("XLT", xlt), ("WGL", wlm)] if (tt == 0 and ps_ == 0) else (),
                               inc=(MGA, 1) if ps_ == 2 else None)
                kmm_done[tq * 16 + k] = p.val(MGA)
            for tt in range(4):
                gt = tq * 4 + tt
                p.post("vector", lambda e, tt=tt, gt=gt: e.tensor_add(
                    out=gate[:, gt * 8:(gt + 1) * 8], in0=ps_ga[tt][:], in1=bgrep[:]),
                    inc=(GEV, 1), waits=[("MGA", (tq + 1) * 64), ("LDM", LDM_ALL)])
        GEV_ALL = p.val(GEV)

        if "T" not in phases:
            p.post("sync", lambda e: None, waits=[("GEV", GEV_ALL), ("WCD", WCD_ALL)])
            p.build(ctx)
            return nc
        for gt in range(16):
            gsl = gate[:, gt * 8:(gt + 1) * 8]
            for j in range(8):
                p.post("vector", lambda e, gsl=gsl, gt=gt, j=j: e.tensor_single_scalar(
                    out=sel[:], in_=gsl, scalar=gate[:, gt * 8 + j: gt * 8 + j + 1],
                    op=ALU.is_gt), inc=(TKV, 1),
                    waits=[("GEV", GEV_ALL), ("TKV", p.val(TKV))])
                p.post("vector", lambda e, j=j: e.reduce_sum(
                    rk[:, j:j + 1], sel[:], axis=mybir.AxisListType.X),
                    inc=(TKV, 1), waits=[("TKV", p.val(TKV))])
                if j > 0:
                    p.post("vector", lambda e, gsl=gsl, gt=gt, j=j: e.tensor_single_scalar(
                        out=sel[:, 0:j], in_=gsl[:, 0:j],
                        scalar=gate[:, gt * 8 + j: gt * 8 + j + 1], op=ALU.is_equal),
                        inc=(TKV, 1), waits=[("TKV", p.val(TKV))])
                    p.post("vector", lambda e, j=j: e.reduce_sum(
                        esel[:, 0:1], sel[:, 0:j], axis=mybir.AxisListType.X),
                        inc=(TKV, 1), waits=[("TKV", p.val(TKV))])
                    p.post("vector", lambda e, j=j: e.tensor_add(
                        out=rk[:, j:j + 1], in0=rk[:, j:j + 1], in1=esel[:, 0:1]),
                        inc=(TKV, 1), waits=[("TKV", p.val(TKV))])
            p.post("vector", lambda e: e.tensor_single_scalar(
                out=sel[:], in_=rk[:], scalar=6.0, op=ALU.is_lt),
                inc=(TKV, 1), waits=[("TKV", p.val(TKV))])
            p.post("vector", lambda e, gsl=gsl: e.reduce_max(
                stat8[:, 0:1], gsl, axis=mybir.AxisListType.X),
                inc=(TKV, 1), waits=[("TKV", p.val(TKV))])
            p.post("vector", lambda e: e.tensor_single_scalar(
                out=stat8[:, 1:2], in_=stat8[:, 0:1], scalar=-1.0, op=ALU.mult),
                inc=(TKV, 1), waits=[("TKV", p.val(TKV))])
            st_ = p.post("vector", lambda e: e.tensor_copy(
                out=settle8[:, 0:128], in_=gate[:, 0:128]),
                inc=(TKV, 1), waits=[("TKV", p.val(TKV))])
            exg = p.post("scalar", lambda e, gsl=gsl: e.activation(
                out=esel[:], in_=gsl, func=AF.Exp, bias=stat8[:, 1:2]),
                inc=(TKS, 1), waits=[("TKV", st_)])
            exg = p.post("scalar", lambda e: e.activation(
                out=settle8[:, 0:128], in_=gate[:, 0:128], func=AF.Identity),
                inc=(TKS, 1), waits=[("TKS", exg)])
            p.post("vector", lambda e: e.tensor_mul(
                out=esel[:], in0=esel[:], in1=sel[:]),
                inc=(TKV, 1), waits=[("TKS", exg)])
            p.post("vector", lambda e: e.reduce_sum(
                stat8[:, 2:3], esel[:], axis=mybir.AxisListType.X),
                inc=(TKV, 1), waits=[("TKV", p.val(TKV))])
            p.post("vector", lambda e: e.reciprocal(
                out=stat8[:, 3:4], in_=stat8[:, 2:3]),
                inc=(TKV, 1), waits=[("TKV", p.val(TKV))])
            p.post("vector", lambda e: e.tensor_single_scalar(
                out=wsb[:], in_=esel[:], scalar=stat8[:, 3:4], op=ALU.mult),
                inc=(TKV, 1), waits=[("TKV", p.val(TKV))])
            cb = p.post("vector", lambda e: e.tensor_copy(
                out=wbpad[:, 0:8], in_=wsb[:]), inc=(TKV, 1),
                waits=[("TKV", p.val(TKV)), ("WTR", p.val(WTR)), ("WBP", 1)])
            cb2 = p.post("vector", lambda e: e.tensor_copy(
                out=settle8[:, 0:128], in_=gate[:, 0:128]), inc=(TKV, 1),
                waits=[("TKV", cb)])
            tr = p.post("tensor", lambda e: e.transpose(
                ps_w[:], wbpad[:], idn8[:]),
                inc=(WTR, 1), waits=[("TKV", cb2), ("LDM", LDM_ALL), ("WEV", p.val(WEV))])
            evw = p.post("vector", lambda e: e.tensor_copy(
                out=wts[0:8, :], in_=ps_w[0:8, :]),
                inc=(WEV, 1), waits=[("WTR", tr), ("WDMA", p.val(WDMA))])
            evw = p.post("vector", lambda e: e.tensor_copy(
                out=settle8[:, 0:128], in_=gate[:, 0:128]), inc=(WEV, 1),
                waits=[("WEV", evw)])
            p.post("sync", lambda e, gt=gt: e.dma_start(
                out=wt_dram[:, gt * P:(gt + 1) * P], in_=wts[0:8, :]),
                inc=(WDMA, 16), waits=[("WEV", evw)])
        WDMA_ALL = p.val(WDMA)
        p.post("gpsimd", lambda e: e.dma_start(
            out=w3[:], in_=wt_dram[0:3, :]), inc=(WBC, 16),
            waits=[("WDMA", WDMA_ALL)])
        WBC_ALL = p.val(WBC)

        if "H" not in phases:
            p.post("sync", lambda e: None, waits=[("WBC", WBC_ALL), ("WCD", WCD_ALL)])
            p.build(ctx)
            return nc
        for k in range(16):
            p.post("sync", lambda e, k=k: e.dma_start_transpose(
                out=bigB[:, k * T:(k + 1) * T], in_=x3hi_dram[:, k * P:(k + 1) * P]),
                inc=(XT, 16),
                waits=[("XD", XD_ALL), ("XLT", p.val(XLT)), ("MGA", p.val(MGA))] if k == 0 else ())
        XT_ALL = p.val(XT)

        for th in range(2):
            for blk in range(32):
                par = blk % 2
                if blk < 24:
                    e_, hc = blk // 8, blk % 8
                else:
                    e_, hc = None, blk - 24
                if blk < 24 and hc == 0:
                    p.post("gpsimd", lambda e, e_=e_: e.dma_start(
                        out=wrep1[:], in_=wt_dram[e_:e_ + 1, :].partition_broadcast(P)),
                        inc=(WRL, 16),
                        waits=[("WDMA", WDMA_ALL), ("HVV", p.val(HVV))])
                wrl = p.val(WRL)
                wv = [("MGU", p.val(MGU))] if (th > 0 or blk >= 2) else [("WCD", WCD_ALL)]
                if blk < 24:
                    grow = e_ * 16
                    urow = 48 + e_ * 16
                    p.post("gpsimd", lambda e, grow=grow, hc=hc, par=par: e.dma_start(
                        out=wstr[par][:].rearrange("p (k c) -> p k c", k=16),
                        in_=wgu_dram.rearrange("(kc pp) c -> pp kc c", pp=P)[
                            :, grow:grow + 16, hc * P:(hc + 1) * P]),
                        inc=(LWS, 16), waits=wv)
                    p.post("gpsimd", lambda e, urow=urow, hc=hc, par=par: e.dma_start(
                        out=wstr2[par][:].rearrange("p (k c) -> p k c", k=16),
                        in_=wgu_dram.rearrange("(kc pp) c -> pp kc c", pp=P)[
                            :, urow:urow + 16, hc * P:(hc + 1) * P]),
                        inc=(LWS, 16))
                else:
                    p.post("gpsimd", lambda e, hc=hc, par=par: e.dma_start(
                        out=wstr[par][:].rearrange("p (k c) -> p k c", k=16),
                        in_=wgu_dram.rearrange("(kc pp) c -> pp kc c", pp=P)[
                            :, 96:112, hc * P:(hc + 1) * P]),
                        inc=(LWS, 16), waits=wv)
                lws = p.val(LWS)
                for tc in range(2):
                    un = (th * 32 + blk) * 2 + tc
                    tg = th * 1024 + tc * TC
                    pg = ps_g8[0]
                    pu = ps_g8[1]
                    w0 = [("LWS", lws), ("XT", XT_ALL)]
                    if un >= 1:
                        w0.append(("SGU", p.val(SGU)))
                    for k in range(16):
                        p.post("tensor", lambda e, pg=pg, par=par, k=k, tg=tg: e.matmul(
                            pg[:], wstr[par][:, k * 128:(k + 1) * 128],
                            bigB[:, k * T + tg: k * T + tg + TC],
                            start=(k == 0), stop=(k == 15)),
                            waits=w0 if k == 0 else (),
                            inc=(MGU, 1) if k == 15 else None)
                    if blk < 24:
                        w1 = [("HVV", p.val(HVV))] if un >= 1 else []
                        for k in range(16):
                            p.post("tensor", lambda e, pu=pu, par=par, k=k, tg=tg: e.matmul(
                                pu[:], wstr2[par][:, k * 128:(k + 1) * 128],
                                bigB[:, k * T + tg: k * T + tg + TC],
                                start=(k == 0), stop=(k == 15)),
                                waits=w1 if k == 0 else (),
                                inc=(MGU, 1) if k == 15 else None)
                    mgu = p.val(MGU)
                    if blk < 24:
                        s1 = p.post("scalar", lambda e, pg=pg, e_=e_, hc=hc, un=un: e.activation(
                            out=hgst[un % 2][:], in_=pg[:], func=AF.Silu,
                            bias=bges[:, e_ * 8 + hc: e_ * 8 + hc + 1]),
                            inc=(SGU, 1), waits=[("MGU", mgu)])
                        s2 = p.post("scalar", lambda e, pu=pu, e_=e_, hc=hc, un=un: e.activation(
                            out=hust[un % 2][:], in_=pu[:], func=AF.Identity,
                            bias=bues[:, e_ * 8 + hc: e_ * 8 + hc + 1]),
                            inc=(SGU, 1), waits=[("SGU", s1)])
                        h1 = p.post("vector", lambda e, un=un: e.tensor_mul(
                            out=hgst[un % 2][:], in0=hgst[un % 2][:], in1=hust[un % 2][:]),
                            inc=(HVV, 1), waits=[("SGU", s2)])
                        p.post("vector", lambda e, un=un, blk=blk, tc=tc, e_=e_, tg=tg: e.tensor_mul(
                            out=bigA[:, blk * 1024 + tc * TC: blk * 1024 + (tc + 1) * TC],
                            in0=hgst[un % 2][:], in1=wrep1[:, tg:tg + TC]),
                            inc=(HVV, 1), waits=[("HVV", h1), ("WRL", wrl)])
                    else:
                        s1 = p.post("scalar", lambda e, pg=pg, hc=hc, un=un: e.activation(
                            out=hgst[un % 2][:], in_=pg[:], func=AF.Silu,
                            bias=bsus[:, hc:hc + 1]),
                            inc=(SGU, 1), waits=[("MGU", mgu)])
                        p.post("vector", lambda e, un=un, blk=blk, tc=tc: e.tensor_copy(
                            out=bigA[:, blk * 1024 + tc * TC: blk * 1024 + (tc + 1) * TC],
                            in_=hgst[un % 2][:]),
                            inc=(HVV, 1), waits=[("SGU", s1)])
            HVV_TH = p.val(HVV)
            if "D" not in phases:
                continue
            for do in range(16):
                par = do % 2
                wv = [("WCD", WCD_ALL), ("HVV", HVV_TH)]
                if th > 0 or do >= 2:
                    wv.append(("MDN", p.val(MDN)))
                p.post("gpsimd", lambda e, do=do, par=par: e.dma_start(
                    out=wstr[par][:].rearrange("p (k c) -> p k c", k=16),
                    in_=wdb_dram.rearrange("(hc pp) c -> pp hc c", pp=P)[
                        :, 0:16, do * P:(do + 1) * P]),
                    inc=(LDD, 16), waits=wv)
                p.post("gpsimd", lambda e, do=do, par=par: e.dma_start(
                    out=wstr2[par][:].rearrange("p (k c) -> p k c", k=16),
                    in_=wdb_dram.rearrange("(hc pp) c -> pp hc c", pp=P)[
                        :, 16:32, do * P:(do + 1) * P]),
                    inc=(LDD, 16))
                ldd = p.val(LDD)
                for tc in range(2):
                    un = (th * 16 + do) * 2 + tc
                    tg = th * 1024 + tc * TC
                    w0 = [("LDD", ldd), ("WBC", WBC_ALL)]
                    if un >= 1:
                        w0.append(("DEV", p.val(DEV)))
                    p.post("tensor", lambda e, do=do, tg=tg: e.matmul(
                        ps_d8[:], bd3s[:, do * P:(do + 1) * P], w3[:, tg:tg + TC],
                        start=True, stop=False, skip_group_check=True),
                        waits=w0)
                    p.post("tensor", lambda e, do=do, tg=tg: e.matmul(
                        ps_d8[:], bsds[:, do * P:(do + 1) * P], oness[:, tg:tg + TC],
                        start=False, stop=False, skip_group_check=True))
                    for hc in range(32):
                        buf = wstr[par] if hc < 16 else wstr2[par]
                        hcc = hc % 16
                        p.post("tensor", lambda e, buf=buf, hcc=hcc, hc=hc, tc=tc: e.matmul(
                            ps_d8[:], buf[:, hcc * 128:(hcc + 1) * 128],
                            bigA[:, hc * 1024 + tc * TC: hc * 1024 + (tc + 1) * TC],
                            start=False, stop=(hc == 31), skip_group_check=True),
                            inc=(MDN, 1) if hc == 31 else None)
                    dv = p.post("vector", lambda e, un=un: e.tensor_copy(
                        out=mpst[un % 2][:], in_=ps_d8[:]),
                        inc=(DEV, 1), waits=[("MDN", p.val(MDN))] +
                        ([("STM", (un - 1) * 16)] if un >= 2 else []))
                    p.post("sync", lambda e, do=do, tg=tg, un=un: e.dma_start(
                        out=o_mp[do * P:(do + 1) * P, tg:tg + TC],
                        in_=mpst[un % 2][:]),
                        inc=(STM, 16), waits=[("DEV", dv)])
        if "D" in phases:
            p.post("sync", lambda e: None, waits=[("STM", p.val(STM))])
        else:
            p.post("sync", lambda e: None, waits=[("HVV", p.val(HVV))])
        p.build(ctx)
    return nc


def host_inputs_launch8(inp, c, l7outs):
    b, s = c // 2, c % 2
    even, odd = 2 * (c // 2), 2 * (c // 2) + 1
    asc = np.ascontiguousarray
    my_e = [3 * s, 3 * s + 1, 3 * s + 2]
    oth_e = [3 * (1 - s), 3 * (1 - s) + 1, 3 * (1 - s) + 2]
    perm = my_e + oth_e + [6, 7]
    Wg = inp["L1_ffn_Wg"][:, perm]
    bg = inp["L1_ffn_bg"][perm]
    weg = np.concatenate([inp["L1_ffn_Wgate"][e] for e in my_e], 0)
    weu = np.concatenate([inp["L1_ffn_Wup"][e] for e in my_e], 0)
    wed = np.concatenate([inp["L1_ffn_Wdown"][e] for e in my_e], 0)
    bge = np.stack([percol(inp["L1_ffn_bgate"][e]) for e in my_e], 0)
    bue = np.stack([percol(inp["L1_ffn_bup"][e]) for e in my_e], 0)
    bd3 = np.stack([inp["L1_ffn_bdown"][e] for e in my_e], 0)
    bsd = inp["L1_ffn_bs_down"][None, :] if s == 0 else np.zeros((1, DM), np.float32)
    return dict(
        x3a=l7outs[even]["x3o"], x3b=l7outs[odd]["x3o"],
        wgp=asc(Wg), bgp=asc(bg[None, :]),
        weg=asc(weg), weu=asc(weu), wed=asc(wed),
        wsu=asc(inp["L1_ffn_Ws_up"][:, s * 1024:(s + 1) * 1024]),
        wsd=asc(inp["L1_ffn_Ws_down"][s * 1024:(s + 1) * 1024, :]),
        bge=asc(bge.transpose(1, 0, 2).reshape(P, 24)),
        bue=asc(bue.transpose(1, 0, 2).reshape(P, 24)),
        bsu=percol(inp["L1_ffn_bs_up"][s * 1024:(s + 1) * 1024]),
        bd3=asc(bd3), bsd=asc(bsd),
        ones=np.ones((1, T), np.float32).astype(BF),
        idin=np.eye(P, dtype=np.float32).astype(BF),
    )


def build_launch9():
    nc = bass.Bass("TRN2", target_bir_lowering=False, debug=False)
    mpa = nc.dram_tensor("mpa", [DM, 1024], BF16, kind="ExternalInput").ap()
    mpb = nc.dram_tensor("mpb", [DM, 1024], BF16, kind="ExternalInput").ap()
    x3own = nc.dram_tensor("x3own", [1024, DM], F32, kind="ExternalInput").ap()
    x4o = nc.dram_tensor("x4o", [1024, DM], F32, kind="ExternalOutput").ap()

    with ExitStack() as ctx:
        ma = [sbuf(nc, ctx, f"ma{i}", [P, DM], BF16) for i in range(2)]
        mb = [sbuf(nc, ctx, f"mb{i}", [P, DM], BF16) for i in range(2)]
        x3s = [sbuf(nc, ctx, f"x3s{i}", [P, DM], F32) for i in range(2)]
        acc = [sbuf(nc, ctx, f"acc{i}", [P, DM], F32) for i in range(2)]
        x4t = [sbuf(nc, ctx, f"x4t{i}", [P, DM], F32) for i in range(2)]
        trash = sbuf(nc, ctx, "trash", [P, DM], F32)
        stat = sbuf(nc, ctx, "stat", [P, 8], F32)
        p = Prog(nc)
        LD = "LD"; XT9 = "XT9"; LNV = "LNV"; LNS = "LNS"; STX = "STX"
        for tt in range(8):
            par = tt % 2
            wv = [("LNV", (tt - 1) * 9)] if tt >= 2 else []
            t1 = p.post("sync", lambda e, tt=tt, par=par: e.dma_start_transpose(
                out=ma[par][:], in_=mpa[:, tt * P:(tt + 1) * P]),
                inc=(XT9, 16), waits=wv)
            t2 = p.post("sync", lambda e, tt=tt, par=par: e.dma_start_transpose(
                out=mb[par][:], in_=mpb[:, tt * P:(tt + 1) * P]),
                inc=(XT9, 16))
            l1 = p.post("gpsimd", lambda e, tt=tt, par=par: e.dma_start(
                out=x3s[par][:], in_=x3own[tt * P:(tt + 1) * P, :]),
                inc=(LD, 16), waits=wv)
            a1 = p.post("vector", lambda e, par=par: e.tensor_add(
                out=acc[par][:], in0=ma[par][:], in1=mb[par][:]),
                inc=(LNV, 1), waits=[("XT9", p.val(XT9)), ("LD", p.val(LD))])
            a2 = p.post("vector", lambda e, par=par: e.tensor_add(
                out=acc[par][:], in0=acc[par][:], in1=x3s[par][:]),
                inc=(LNV, 1), waits=[("LNV", a1)])
            r1 = p.post("vector", lambda e, par=par: e.reduce_sum(
                stat[:, 0:1], acc[par][:], axis=mybir.AxisListType.X),
                inc=(LNV, 1), waits=[("LNV", a2)])
            r2 = p.post("vector", lambda e: e.tensor_single_scalar(
                out=stat[:, 1:2], in_=stat[:, 0:1], scalar=-1.0 / DM, op=ALU.mult),
                inc=(LNV, 1), waits=[("LNV", r1)])
            r3 = p.post("vector", lambda e: e.tensor_copy(
                out=trash[:, 0:256], in_=acc[0][:, 0:256]),
                inc=(LNV, 1), waits=[("LNV", r2)])
            q1 = p.post("scalar", lambda e, par=par: e.activation(
                out=trash[:], in_=acc[par][:], func=AF.Square, bias=stat[:, 1:2]),
                inc=(LNS, 1), waits=[("LNV", r3)] +
                ([("LNV", (tt - 1) * 9 + 6)] if tt >= 1 else []))
            r4 = p.post("vector", lambda e: e.reduce_sum(
                stat[:, 2:3], trash[:], axis=mybir.AxisListType.X),
                inc=(LNV, 1), waits=[("LNS", q1)])
            r5 = p.post("vector", lambda e: e.tensor_scalar(
                out=stat[:, 3:4], in0=stat[:, 2:3], scalar1=1.0 / DM, scalar2=1e-5,
                op0=ALU.mult, op1=ALU.add),
                inc=(LNV, 1), waits=[("LNV", r4)])
            r6 = p.post("vector", lambda e: e.tensor_copy(
                out=trash[:, 0:256], in_=acc[0][:, 0:256]),
                inc=(LNV, 1), waits=[("LNV", r5)])
            q2 = p.post("scalar", lambda e: e.activation(
                out=stat[:, 4:5], in_=stat[:, 3:4], func=AF.Ln),
                inc=(LNS, 1), waits=[("LNV", r6)])
            r7 = p.post("vector", lambda e: e.tensor_single_scalar(
                out=stat[:, 5:6], in_=stat[:, 4:5], scalar=-0.5, op=ALU.mult),
                inc=(LNV, 1), waits=[("LNS", q2)])
            q3 = p.post("scalar", lambda e: e.activation(
                out=stat[:, 6:7], in_=stat[:, 5:6], func=AF.Exp),
                inc=(LNS, 1), waits=[("LNV", r7)])
            f1 = p.post("vector", lambda e, par=par: e.tensor_scalar(
                out=x4t[par][:], in0=acc[par][:], scalar1=stat[:, 1:2],
                scalar2=stat[:, 6:7], op0=ALU.add, op1=ALU.mult),
                inc=("XF", 1), waits=[("LNS", q3)] +
                ([("STX", (tt - 2) * 16 + 16)] if tt >= 2 else []))
            p.post("sync", lambda e, tt=tt, par=par: e.dma_start(
                out=x4o[tt * P:(tt + 1) * P, :], in_=x4t[par][:]),
                inc=(STX, 16), waits=[("XF", f1)])
        p.post("sync", lambda e: None, waits=[("STX", p.val(STX))])
        p.build(ctx)
    return nc


def host_inputs_launch9(inp, c, l7outs, l8outs):
    b, s = c // 2, c % 2
    even, odd = 2 * (c // 2), 2 * (c // 2) + 1
    asc = np.ascontiguousarray
    tsl = slice(s * 1024, (s + 1) * 1024)
    return dict(
        mpa=asc(l8outs[even]["o_mp"][:, tsl]),
        mpb=asc(l8outs[odd]["o_mp"][:, tsl]),
        x3own=l7outs[c]["x3o"],
    )


_BUILD_CACHE = {}


def _get(name, builder):
    if name not in _BUILD_CACHE:
        _BUILD_CACHE[name] = builder()
    return _BUILD_CACHE[name]


def kernel(**inputs):
    cores = list(range(8))

    def run(name, builder, maps):
        nc = _get(name, builder)
        return run_bass_kernel_spmd(nc, maps, core_ids=cores).results

    l1 = run("l1", build_launch1, [host_inputs_launch1(inputs, c) for c in range(8)])
    l2 = run("l2", build_launch2, [host_inputs_launch2(inputs, c, l1) for c in range(8)])
    l3 = run("l3", build_launch3, [host_inputs_launch3(inputs, c, l2) for c in range(8)])
    l4 = run("l4", build_launch4, [host_inputs_launch4(inputs, c, l3) for c in range(8)])
    l5 = run("l5", build_launch5, [host_inputs_launch5(inputs, c, l3, l4) for c in range(8)])
    l6 = run("l6", build_launch6, [host_inputs_launch6(inputs, c, l5) for c in range(8)])
    l7 = run("l7", build_launch7, [host_inputs_launch7(inputs, c, l5, l6) for c in range(8)])
    l8 = run("l8", build_launch8, [host_inputs_launch8(inputs, c, l7) for c in range(8)])
    l9 = run("l9", build_launch9, [host_inputs_launch9(inputs, c, l7, l8) for c in range(8)])

    out = np.empty((4, 2048, DM), np.float32)
    for c in range(8):
        b, s = c // 2, c % 2
        out[b, s * 1024:(s + 1) * 1024, :] = l9[c]["x4o"]
    return out
